# version 04 (final)
# baseline (speedup 1.0000x reference)
import numpy as np
import ml_dtypes
from contextlib import ExitStack

import concourse.bass as bass
import concourse.mybir as mybir
from concourse.bass_utils import run_bass_kernel_spmd
from concourse.bass_core_rust import IndirectOffsetOnAxis

F32 = mybir.dt.float32
BF16 = mybir.dt.bfloat16
I32 = mybir.dt.int32
AF = mybir.ActivationFunctionType
ALU = mybir.AluOpType
AXX = mybir.AxisListType.X
BFNP = ml_dtypes.bfloat16

B, T, D, H, DH, NE, HID = 2, 8192, 1024, 16, 64, 8, 4096
CAP = 2048
NCORE = 8
SL = 2048
NT = SL // 128
NF = D // 128
EPS_LN = 1e-5
EPS_Z = 1e-6
BIGIDX = float(1 << 22)


def consts_np():
    ident = np.eye(128, dtype=BFNP)
    maskT = np.triu(np.ones((128, 128), np.float32)).astype(BFNP)
    blockpat = np.kron(np.eye(H, dtype=np.float32),
                       np.ones((DH, 1), np.float32)).astype(BFNP)
    E = np.zeros((H, D), np.float32)
    for m in range(NF):
        for p in range(128):
            E[2 * m + p // 64, 128 * m + p] = 1.0
    return ident, maskT, blockpat, E.astype(BFNP)


class Sem:
    def __init__(self, C, name):
        self.h = C.es.enter_context(C.nc.semaphore(name))
        self.v = 0

    def inc(self, instr, k=1):
        instr.then_inc(self.h, k)
        self.v += k
        return self.v


class Ctx:
    def __init__(self, nc, es, pref=""):
        self.nc, self.es, self.pref = nc, es, pref

    def sb(self, name, shape, dtype):
        return self.es.enter_context(
            self.nc.sbuf_tensor(self.pref + name, shape, dtype)).ap()

    def ps(self, name, shape, dtype=F32):
        return self.es.enter_context(
            self.nc.psum_tensor(self.pref + name, shape, dtype)).ap()

    def sem(self, name):
        return Sem(self, self.pref + name)



class Chain:

    def __init__(self, C, name):
        self.s = C.sem(name)

    def bar(self, eng, ins):
        self.s.inc(ins)
        eng.wait_ge(self.s.h, self.s.v)


def emit_L1(nc, xs, wq_d, wk_d, wv_d, qz_o, kf_o, ktm_o, vtm_o, ss_o,
            ident_c, bp_c, E_c, pref=""):
    with ExitStack() as es:
        C = Ctx(nc, es, pref)
        wbuf = [C.sb(f"w{i}", [128, NF * 1024], BF16) for i in range(2)]
        xh = C.sb("xh", [128, 4 * 1024], F32)
        xn_tm = C.sb("xn_tm", [128, NT * 1024], BF16)
        scrA = C.sb("scrA", [128, 4096], F32)
        scrA_bf = scrA.tensor.bitcast(BF16).ap()
        fa = scrA[:, 0:512]
        fe = scrA[:, 512:1024]
        zz = scrA[0:16, 0:2048]
        zbb = scrA_bf[0:16, 4096:6144]
        stat = C.sb("stat", [128, 64], F32)
        sq4 = scrA
        cns = C.sb("cns", [128, 2], F32)
        fscr = C.sb("fscr", [128, 2], F32)
        xn_fm = [C.sb(f"xnf{k}", [128, SL], BF16) for k in range(NF)]
        qf = [C.sb(f"qf{k}", [128, SL], BF16) for k in range(NF)]
        kf = [C.sb(f"kf{k}", [128, SL], BF16) for k in range(NF)]
        vrot = [C.sb(f"vr{i}", [128, D], BF16) for i in range(2)]
        ktm_rot = [C.sb(f"ktr{i}", [128, D], BF16) for i in range(2)]
        ident = C.sb("identS", [128, 128], BF16)
        bp = C.sb("bpS", [128, 16 * NF], BF16)
        Ehb = C.sb("EhbS", [16, D], BF16)
        ssum = C.sb("ssumS", [128, 512], F32)

        ps_mm = [C.ps(f"pmm{i}", [128, 512], F32) for i in range(4)]
        ps_tp = [C.ps(f"ptp{i}", [128, 1024], BF16) for i in range(2)]
        ps_z = C.ps("psz", [16, 512], F32)

        s_c = C.sem("consts")
        s_xh = [C.sem("xh0"), C.sem("xh1")]
        s_w = C.sem("w")
        s_cns = C.sem("cnsS")
        s_st = C.sem("stS")
        s_sd = C.sem("sdS")
        s_xn = C.sem("xnS")
        s_tp = C.sem("tpS")
        s_te = C.sem("teS")
        chv = Chain(C, "chvA")

        with nc.Block() as blk:
            @blk.sync
            def _(sy):
                sy.dma_start(out=ident, in_=ident_c).then_inc(s_c.h, 16)
                for k in range(NF):
                    sy.dma_start(out=bp[:, 16 * k:16 * (k + 1)],
                                 in_=bp_c[128 * k:128 * (k + 1), :]
                                 ).then_inc(s_c.h, 16)
                sy.dma_start(out=Ehb, in_=E_c).then_inc(s_c.h, 16)
                s_c.v += 160
                for hf in range(4):
                    if hf >= 1:
                        sy.wait_ge(s_xn.h, hf)
                    for j in range(4):
                        r0 = 512 * hf + 128 * j
                        sy.dma_start(out=xh[:, 1024 * j:1024 * (j + 1)],
                                     in_=xs[r0:r0 + 128, :]
                                     ).then_inc(s_xh[hf % 2].h, 16)
                        s_xh[hf % 2].v += 16

            @blk.gpsimd
            def _(g):
                g.dma_start(out=wbuf[0].rearrange("p (k m) -> p k m", m=1024),
                            in_=wq_d.rearrange("(k p) m -> p k m", p=128)
                            ).then_inc(s_w.h, 16)
                g.dma_start(out=wbuf[1].rearrange("p (k m) -> p k m", m=1024),
                            in_=wk_d.rearrange("(k p) m -> p k m", p=128)
                            ).then_inc(s_w.h, 16)
                s_w.v += 32
                g.memset(cns[:, 0:1], EPS_LN)
                g.memset(cns[:, 1:2], 2.0 ** -30).then_inc(s_cns.h, 1)
                s_cns.v += 1
                g.wait_ge(s_w.h, 32)

            @blk.vector
            def _(v):
                for hf in range(4):
                    c0 = 4 * hf
                    v.wait_ge(s_xh[hf % 2].h, 64 * (hf // 2 + 1))
                    for j in range(4):
                        ins = v.reduce_sum(stat[:, c0 + j:c0 + j + 1],
                                           xh[:, 1024 * j:1024 * (j + 1)],
                                           axis=AXX)
                    chv.bar(v, ins)
                    for j in range(4):
                        ins = v.tensor_scalar(
                            xh[:, 1024 * j:1024 * (j + 1)],
                            xh[:, 1024 * j:1024 * (j + 1)],
                            1024.0, stat[:, c0 + j:c0 + j + 1],
                            ALU.mult, ALU.subtract)
                    chv.bar(v, ins)
                    for j in range(4):
                        ins = v.tensor_mul(sq4[:, 1024 * j:1024 * (j + 1)],
                                           xh[:, 1024 * j:1024 * (j + 1)],
                                           xh[:, 1024 * j:1024 * (j + 1)])
                    chv.bar(v, ins)
                    for j in range(4):
                        i = c0 + j
                        ins = v.reduce_sum(stat[:, 16 + i:16 + i + 1],
                                           sq4[:, 1024 * j:1024 * (j + 1)],
                                           axis=AXX)
                    s_st.inc(ins)
                    v.wait_ge(s_sd.h, hf + 1)
                    ins = v.reciprocal(stat[:, 48 + c0:48 + c0 + 4],
                                       stat[:, 32 + c0:32 + c0 + 4])
                    chv.bar(v, ins)
                    for j in range(4):
                        i = c0 + j
                        ins = v.tensor_scalar(
                            xn_tm[:, 1024 * i:1024 * (i + 1)],
                            xh[:, 1024 * j:1024 * (j + 1)],
                            stat[:, 48 + i:48 + i + 1], 1.0 / 1024.0,
                            ALU.mult, ALU.mult)
                    s_xn.inc(ins)

            @blk.scalar
            def _(s):
                for hf in range(4):
                    c0 = 4 * hf
                    s.wait_ge(s_st.h, hf + 1)
                    if hf == 0:
                        s.wait_ge(s_cns.h, 1)
                    s_sd.inc(s.activation(out=stat[:, 32 + c0:32 + c0 + 4],
                                          in_=stat[:, 16 + c0:16 + c0 + 4],
                                          func=AF.Sqrt, bias=cns[:, 0:1],
                                          scale=cns[:, 1:2]))
                for i in range(NT):
                    s.wait_ge(s_tp.h, i + 1)
                    for k in range(NF):
                        ins = s.activation(
                            out=xn_fm[k][:, 128 * i:128 * (i + 1)],
                            in_=ps_tp[i % 2][:, 128 * k:128 * (k + 1)],
                            func=AF.Copy)
                    s_te.inc(ins)

            @blk.tensor
            def _(t):
                t.wait_ge(s_c.h, 160)
                for i in range(NT):
                    t.wait_ge(s_xn.h, i // 4 + 1)
                    if i >= 2:
                        t.wait_ge(s_te.h, i - 1)
                    for k in range(NF):
                        ins = t.transpose(
                            ps_tp[i % 2][:, 128 * k:128 * (k + 1)],
                            xn_tm[:, 1024 * i + 128 * k:1024 * i + 128 * (k + 1)],
                            ident)
                    s_tp.inc(ins)
        nc.all_engine_barrier()

        s_pe = C.sem("peB")
        s_ev = C.sem("evB")
        s_ev2 = C.sem("evB2")
        s_fa = C.sem("faB")
        s_fe = C.sem("feB")
        s_w2 = C.sem("wB")
        s_vo = [C.sem("vo0"), C.sem("vo1")]

        groups_q = [(m, n) for m in range(NF) for n in range(SL // 512)]
        NG = len(groups_q)

        with nc.Block() as blk:
            @blk.gpsimd
            def _(g):
                g.wait_ge(s_pe.h, NG)
                g.dma_start(out=wbuf[0].rearrange("p (k m) -> p k m", m=1024),
                            in_=wv_d.rearrange("(k p) m -> p k m", p=128)
                            ).then_inc(s_w2.h, 16)
                s_w2.v += 16
                for mt in range(NT):
                    g.wait_ge(s_ev2.h, 2 * (mt + 1))
                    g.dma_start(out=vtm_o[128 * mt:128 * (mt + 1), :],
                                in_=vrot[mt % 2]).then_inc(s_vo[mt % 2].h, 16)
                    s_vo[mt % 2].v += 16
                g.wait_ge(s_vo[0].h, s_vo[0].v)
                g.wait_ge(s_vo[1].h, s_vo[1].v)

            @blk.tensor
            def _(t):
                gi = 0
                for src in (0, 1):
                    for (m, n) in groups_q:
                        if gi >= 4:
                            t.wait_ge(s_ev.h, gi - 3)
                        for k in range(NF):
                            ins = t.matmul(
                                ps_mm[gi % 4],
                                wbuf[src][:, 1024 * k + 128 * m:
                                          1024 * k + 128 * (m + 1)],
                                xn_fm[k][:, 512 * n:512 * (n + 1)],
                                start=(k == 0), stop=(k == NF - 1))
                        s_pe.inc(ins)
                        gi += 1
                t.wait_ge(s_w2.h, 16)
                for mt in range(NT):
                    for n in range(2):
                        if gi - 4 < 2 * NG:
                            t.wait_ge(s_ev.h, gi - 3)
                        else:
                            t.wait_ge(s_ev2.h, gi - 3 - 2 * NG)
                        for k in range(NF):
                            ins = t.matmul(
                                ps_mm[gi % 4],
                                xn_fm[k][:, 128 * mt:128 * (mt + 1)],
                                wbuf[0][:, 1024 * k + 512 * n:
                                        1024 * k + 512 * (n + 1)],
                                start=(k == 0), stop=(k == NF - 1))
                        s_pe.inc(ins)
                        gi += 1

            @blk.vector
            def _(v):
                gi = 0
                for dstbuf in (qf, kf):
                    for (m, n) in groups_q:
                        v.wait_ge(s_pe.h, gi + 1)
                        s_fa.inc(v.tensor_scalar_min(fa, ps_mm[gi % 4], 0.0))
                        v.wait_ge(s_fe.h, gi + 1)
                        s_ev.inc(v.scalar_tensor_tensor(
                            dstbuf[m][:, 512 * n:512 * (n + 1)],
                            ps_mm[gi % 4], 0.0, fe, ALU.max, ALU.add))
                        gi += 1

            @blk.scalar
            def _(s):
                for gi in range(2 * NG):
                    s.wait_ge(s_fa.h, gi + 1)
                    s_fe.inc(s.activation(out=fe, in_=fa, func=AF.Exp))
                vg = 2 * NG
                for mt in range(NT):
                    for n in range(2):
                        s.wait_ge(s_pe.h, vg + 1)
                        if mt >= 2 and n == 0:
                            s.wait_ge(s_vo[mt % 2].h, 16 * (mt // 2))
                        s_ev2.inc(s.activation(
                            out=vrot[mt % 2][:, 512 * n:512 * (n + 1)],
                            in_=ps_mm[vg % 4], func=AF.Copy))
                        vg += 1
        nc.all_engine_barrier()

        s_tp2 = C.sem("tpC")
        s_te2 = C.sem("teC")
        s_sc = C.sem("scC")
        s_sa = C.sem("saC")
        s_go = [C.sem("go0"), C.sem("go1")]
        s_do = C.sem("doC")
        s_sf = C.sem("sfC")
        s_vl = [C.sem("vl0"), C.sem("vl1")]
        chc = Chain(C, "chvC")

        with nc.Block() as blk:
            @blk.scalar
            def _(s):
                for k in range(NF):
                    s.dma_start(out=kf_o[128 * k:128 * (k + 1), :],
                                in_=kf[k]).then_inc(s_do.h, 16)
                    s_do.v += 16
                for c in range(NT):
                    s.wait_ge(s_tp2.h, c + 1)
                    if c >= 2:
                        s.wait_ge(s_sc.h, c - 1)
                        s.wait_ge(s_go[c % 2].h, 16 * (c // 2))
                    s_te2.inc(s.activation(out=ktm_rot[c % 2], in_=ps_tp[c % 2],
                                           func=AF.Copy))
                s.wait_ge(s_do.h, s_do.v)

            @blk.sync
            def _(sy):
                for c in range(NT):
                    if c >= 2:
                        sy.wait_ge(s_sc.h, c - 1)
                    sy.dma_start(out=vrot[c % 2],
                                 in_=vtm_o[128 * c:128 * (c + 1), :]
                                 ).then_inc(s_vl[c % 2].h, 16)
                    s_vl[c % 2].v += 16

            @blk.tensor
            def _(t):
                for c in range(NT):
                    if c >= 2:
                        t.wait_ge(s_te2.h, c - 1)
                    for k in range(NF):
                        ins = t.transpose(ps_tp[c % 2][:, 128 * k:128 * (k + 1)],
                                          kf[k][:, 128 * c:128 * (c + 1)], ident)
                    s_tp2.inc(ins)
                    t.wait_ge(s_te2.h, c + 1)
                    t.wait_ge(s_vl[c % 2].h, 16 * (c // 2 + 1))
                    if c >= 2:
                        t.wait_ge(s_sa.h, c - 1)
                    for h in range(H):
                        ins = t.matmul(
                            ps_mm[c % 2][64 * (h % 2):64 * (h % 2) + 64,
                                         64 * (h // 2):64 * (h // 2) + 64],
                            ktm_rot[c % 2][:, 64 * h:64 * (h + 1)],
                            vrot[c % 2][:, 64 * h:64 * (h + 1)],
                            start=True, stop=True, skip_group_check=True)
                    s_sc.inc(ins)

            @blk.gpsimd
            def _(g):
                for c in range(NT):
                    g.wait_ge(s_te2.h, c + 1)
                    g.dma_start(out=ktm_o[128 * c:128 * (c + 1), :],
                                in_=ktm_rot[c % 2]).then_inc(s_go[c % 2].h, 16)
                    s_go[c % 2].v += 16
                g.wait_ge(s_go[0].h, s_go[0].v)
                g.wait_ge(s_go[1].h, s_go[1].v)

            @blk.vector
            def _(v):
                for c in range(NT):
                    v.wait_ge(s_sc.h, c + 1)
                    if c == 0:
                        ins = v.tensor_copy(out=ssum, in_=ps_mm[0])
                    else:
                        ins = v.tensor_add(ssum, ssum, ps_mm[c % 2])
                    s_sa.inc(ins)
                    v.wait_ge(s_sa.h, s_sa.v)
                s_sf.inc(v.tensor_copy(out=fscr[:, 0:1], in_=ssum[:, 511:512]))

            @blk.scalar
            def _(s):
                s.wait_ge(s_sf.h, 1)
                s.dma_start(out=ss_o, in_=ssum).then_inc(s_do.h, 16)
                s_do.v += 16
                s.wait_ge(s_do.h, s_do.v)
        nc.all_engine_barrier()

        s_d1 = C.sem("d1")
        s_d2 = C.sem("d2")
        s_d3 = C.sem("d3")
        s_d4 = C.sem("d4")
        s_d5 = C.sem("d5")
        s_d6 = C.sem("d6")
        s_do2 = C.sem("do2")
        chd = Chain(C, "chvD")
        NZG = SL // 512

        with nc.Block() as blk:
            @blk.vector
            def _(v):
                cp0 = scrA_bf[0:64, 0:2048]
                cps = [scrA_bf[0:64, 2048:4096], scrA_bf[0:64, 4096:6144]]
                for m in range(NF):
                    lo = kf[m][0:64, :]
                    hi = kf[m][64:128, :]
                    chd.bar(v, v.tensor_copy(out=cp0, in_=hi))
                    if m > 0:
                        chd.bar(v, v.tensor_add(lo, lo, cps[(m - 1) % 2]))
                    chd.bar(v, v.tensor_add(cps[m % 2], cp0, lo))
                    chd.bar(v, v.tensor_copy(out=hi, in_=cps[m % 2]))
                for m in range(NF):
                    ins = v.tensor_mul(xn_fm[m], qf[m], kf[m])
                s_d1.inc(ins)
                for n in range(NZG):
                    v.wait_ge(s_d2.h, n + 1)
                    ins = v.tensor_scalar_add(zz[:, 512 * n:512 * (n + 1)],
                                              ps_z, EPS_Z)
                    s_d3.inc(ins)
                v.wait_ge(s_d3.h, s_d3.v)
                chd.bar(v, v.reciprocal(zz, zz))
                ins = v.tensor_copy(zbb, zz)
                s_d3.inc(ins)
                gi = 0
                for m in range(NF):
                    for n in range(NZG):
                        v.wait_ge(s_d4.h, gi + 1)
                        ins = v.tensor_mul(qf[m][:, 512 * n:512 * (n + 1)],
                                           qf[m][:, 512 * n:512 * (n + 1)],
                                           ps_mm[gi % 4])
                        s_d5.inc(ins)
                        gi += 1
                    v.wait_ge(s_d5.h, gi)
                    s_d6.inc(v.tensor_copy(out=fscr[:, 1:2],
                                           in_=qf[m].bitcast(F32)[:, 1023:1024]))

            @blk.tensor
            def _(t):
                t.wait_ge(s_d1.h, 1)
                for n in range(NZG):
                    if n > 0:
                        t.wait_ge(s_d3.h, n)
                    for k in range(NF):
                        ins = t.matmul(ps_z, bp[:, 16 * k:16 * (k + 1)],
                                       xn_fm[k][:, 512 * n:512 * (n + 1)],
                                       start=(k == 0), stop=(k == NF - 1))
                    s_d2.inc(ins)
                t.wait_ge(s_d3.h, NZG + 1)
                gi = 0
                for m in range(NF):
                    for n in range(NZG):
                        if gi >= 4:
                            t.wait_ge(s_d5.h, gi - 3)
                        s_d4.inc(t.matmul(ps_mm[gi % 4],
                                          Ehb[:, 128 * m:128 * (m + 1)],
                                          zbb[:, 512 * n:512 * (n + 1)],
                                          start=True, stop=True))
                        gi += 1

            @blk.gpsimd
            def _(g):
                for m in range(NF):
                    g.wait_ge(s_d6.h, m + 1)
                    g.dma_start(out=qz_o[128 * m:128 * (m + 1), :],
                                in_=qf[m]).then_inc(s_do2.h, 16)
                    s_do2.v += 16
                g.wait_ge(s_do2.h, s_do2.v)





def build_L1():
    nc = bass.Bass("TRN2", target_bir_lowering=False, debug=False)
    xs = nc.dram_tensor("xs", [SL, D], F32, kind="ExternalInput").ap()
    wq_d = nc.dram_tensor("wq", [D, D], F32, kind="ExternalInput").ap()
    wk_d = nc.dram_tensor("wk", [D, D], F32, kind="ExternalInput").ap()
    wv_d = nc.dram_tensor("wv", [D, D], F32, kind="ExternalInput").ap()
    qz_o = nc.dram_tensor("qz", [D, SL], BF16, kind="ExternalOutput").ap()
    kf_o = nc.dram_tensor("kf", [D, SL], BF16, kind="ExternalOutput").ap()
    ktm_o = nc.dram_tensor("ktm", [SL, D], BF16, kind="ExternalOutput").ap()
    vtm_o = nc.dram_tensor("vtm", [SL, D], BF16, kind="ExternalOutput").ap()
    ss_o = nc.dram_tensor("ssum", [128, 512], F32, kind="ExternalOutput").ap()
    ident_np, _, blockpat_np, E_np = consts_np()
    ident_c = nc.inline_tensor(ident_np, name="identC").ap()
    bp_c = nc.inline_tensor(blockpat_np, name="bpC").ap()
    E_c = nc.inline_tensor(E_np, name="EC").ap()
    emit_L1(nc, xs, wq_d, wk_d, wv_d, qz_o, kf_o, ktm_o, vtm_o, ss_o,
            ident_c, bp_c, E_c)
    return nc


def np_feature(x):
    return np.where(x > 0, x + 1.0, np.exp(x))


def np_ln(x):
    m = x.mean(-1, keepdims=True)
    v = x.var(-1, keepdims=True)
    return (x - m) / np.sqrt(v + EPS_LN)





def emit_L2(nc, xs, qz_d, kf_d, ktm_d, vtm_d, sall_d, msk_d, wo_d, bo_d,
            gw_d, r1_o, xn2_o, probs_o, ident_c, maskT_c, ones_c, ones8f_c,
            ones18f_c, NSRC=NCORE, pref="", nph=4, sparts=7,
            r1_dst=None, xn2_wr=None, probs_wr=None,
            write_r1=True, write_xn2=True):
    if r1_dst is None:
        r1_dst = lambda i, eng: r1_o[128 * i:128 * (i + 1), :]
    if xn2_wr is None:
        xn2_wr = lambda eng, i, src: eng.dma_start(
            out=xn2_o[128 * i:128 * (i + 1), :], in_=src)
    if probs_wr is None:
        probs_wr = lambda eng, src: eng.dma_start(out=probs_o, in_=src)
    with ExitStack() as es:
        C = Ctx(nc, es, pref)
        kfs = [C.sb(f"kfs{i}", [128, 1024], BF16) for i in range(2)]
        qzs = [C.sb(f"qzs{i}", [128, 1024], BF16) for i in range(2)]
        kts = [C.sb(f"kts{i}", [128, 1024], BF16) for i in range(2)]
        vts = [C.sb(f"vts{i}", [128, 1024], BF16) for i in range(2)]
        attb = C.sb("attb", [128, 2048], BF16)
        sbf = [C.sb(f"sbf{i}", [128, 512], BF16) for i in range(2)]
        Sf32 = C.sb("Sf32", [128, 512], F32)
        sal = C.sb("sal", [128, NSRC * 512], F32)
        msk = C.sb("mskS", [128, NSRC], F32)
        stream = [C.sb(f"st{g}", [128, SL], BF16) for g in range(NF)]
        wo_b = C.sb("wo_b", [128, NF * 1024], BF16)
        bo_b = C.sb("bo_b", [1, D], BF16)
        gw_b = C.sb("gw_b", [128, NF * NE], BF16)
        onesr = C.sb("onesr", [1, 128], BF16)
        ones8 = C.sb("ones8", [8, 1], F32)
        ones18 = C.sb("ones18", [1, 8], F32)
        ident = C.sb("identS", [128, 128], BF16)
        maskT = C.sb("maskTS", [128, 128], BF16)
        xq = C.sb("xq", [128, 4096], F32)
        xcq = C.sb("xcq", [128, 4096], F32)
        xn2q = C.sb("xn2q", [128, 4096], BF16)
        xn2_fm = [C.sb(f"x2f{k}", [128, SL], BF16) for k in range(NF)]
        stat = C.sb("stat", [128, 64], F32)
        cns = C.sb("cns", [128, 2], F32)
        esc = C.sb("esc", [16, SL], F32)
        rec = C.sb("rec", [1, SL], F32)
        prb = C.sb("prb", [16, SL], F32)

        s_c = C.sem("consts")
        s_w = C.sem("w")
        s_cns = C.sem("cnsS")
        s_ini = C.sem("ini")
        s_sb = C.sem("sbS")

        with nc.Block() as blk:
            @blk.sync
            def _(sy):
                sy.dma_start(out=ident, in_=ident_c).then_inc(s_c.h, 16)
                sy.dma_start(out=maskT, in_=maskT_c).then_inc(s_c.h, 16)
                sy.dma_start(out=onesr, in_=ones_c).then_inc(s_c.h, 16)
                sy.dma_start(out=ones8, in_=ones8f_c).then_inc(s_c.h, 16)
                sy.dma_start(out=ones18, in_=ones18f_c).then_inc(s_c.h, 16)
                sy.dma_start(out=msk, in_=msk_d).then_inc(s_c.h, 16)
                sy.dma_start(out=sal.rearrange("p (c s) -> p c s", s=512),
                             in_=sall_d.rearrange("c p s -> p c s")
                             ).then_inc(s_c.h, 16)
                s_c.v += 112

            @blk.gpsimd
            def _(g):
                g.dma_start(out=wo_b.rearrange("p (k m) -> p k m", m=1024),
                            in_=wo_d.rearrange("(k p) m -> p k m", p=128)
                            ).then_inc(s_w.h, 16)
                g.dma_start(out=bo_b, in_=bo_d).then_inc(s_w.h, 16)
                g.dma_start(out=gw_b.rearrange("p (k e) -> p k e", e=NE),
                            in_=gw_d.rearrange("(k p) e -> p k e", p=128)
                            ).then_inc(s_w.h, 16)
                s_w.v += 48
                g.memset(cns[:, 0:1], EPS_LN)
                g.memset(cns[:, 1:2], 2.0 ** -30).then_inc(s_cns.h, 1)
                s_cns.v += 1
                g.wait_ge(s_w.h, s_w.v)

            @blk.vector
            def _(v):
                v.wait_ge(s_c.h, 112)
                ins = v.tensor_scalar(Sf32, sal[:, 0:512], msk[:, 0:1], None,
                                      ALU.mult)
                s_ini.inc(ins)
                v.wait_ge(s_ini.h, s_ini.v)
                for cc in range(1, NSRC):
                    ins = v.scalar_tensor_tensor(
                        Sf32, sal[:, 512 * cc:512 * (cc + 1)],
                        msk[:, cc:cc + 1], Sf32, ALU.mult, ALU.add)
                    s_ini.inc(ins)
                    v.wait_ge(s_ini.h, s_ini.v)
                ins = v.tensor_copy(out=sbf[0], in_=Sf32)
                s_sb.inc(ins)
        nc.all_engine_barrier()

        if nph < 2:
            return
        s_kf = [C.sem("kf0"), C.sem("kf1")]
        s_qz = [C.sem("qzS0"), C.sem("qzS1")]
        s_kt = [C.sem("kt0"), C.sem("kt1")]
        s_vt = [C.sem("vt0"), C.sem("vt1")]
        s_a = C.sem("aS")
        s_m = C.sem("mS")
        s_o = C.sem("oS")
        s_oe = C.sem("oeS")
        s_d = C.sem("dS")
        s_acc = C.sem("accS")

        with ExitStack() as psc:
            P = Ctx(nc, psc, pref)
            ps_att = [P.ps(f"pa{i}", [128, 512], F32) for i in range(2)]
            ps_out = [P.ps(f"po{i}", [128, 1024], F32) for i in range(2)]
            ps_S = [P.ps(f"pS{i}", [128, 512], F32) for i in range(2)]

            with nc.Block() as blk:
                @blk.sync
                def _(sy):
                    for c in range(NT):
                        r = c % 2
                        if c >= 2:
                            sy.wait_ge(s_o.h, c - 1)
                        sy.dma_start(
                            out=kfs[r].rearrange("p (k t) -> p k t", t=128),
                            in_=kf_d.rearrange("(k p) t -> p k t", p=128)
                            [:, :, 128 * c:128 * (c + 1)]
                            ).then_inc(s_kf[r].h, 16)
                        s_kf[r].v += 16
                        sy.dma_start(
                            out=qzs[r].rearrange("p (k t) -> p k t", t=128),
                            in_=qz_d.rearrange("(k p) t -> p k t", p=128)
                            [:, :, 128 * c:128 * (c + 1)]
                            ).then_inc(s_qz[r].h, 16)
                        s_qz[r].v += 16
                        if c >= 2:
                            sy.wait_ge(s_d.h, c - 1)
                        sy.dma_start(out=kts[r],
                                     in_=ktm_d[128 * c:128 * (c + 1), :]
                                     ).then_inc(s_kt[r].h, 16)
                        s_kt[r].v += 16
                        sy.dma_start(out=vts[r],
                                     in_=vtm_d[128 * c:128 * (c + 1), :]
                                     ).then_inc(s_vt[r].h, 16)
                        s_vt[r].v += 16

                @blk.tensor
                def _(t):
                    t.wait_ge(s_c.h, 112)
                    for c in range(NT):
                        r = c % 2
                        t.wait_ge(s_kf[r].h, 16 * (c // 2 + 1))
                        t.wait_ge(s_qz[r].h, 16 * (c // 2 + 1))
                        for half in range(2):
                            G = 2 * c + half
                            if G >= 1:
                                t.wait_ge(s_m.h, G)
                            for j in range(4):
                                for par in range(2):
                                    h = 8 * half + 2 * j + par
                                    ins = t.matmul(
                                        ps_att[par][:, 128 * j:128 * (j + 1)],
                                        kfs[r][64 * par:64 * par + 64,
                                               128 * (h // 2):128 * (h // 2) + 128],
                                        qzs[r][64 * par:64 * par + 64,
                                               128 * (h // 2):128 * (h // 2) + 128],
                                        start=True, stop=True,
                                        skip_group_check=True)
                            s_a.inc(ins)
                        t.wait_ge(s_vt[r].h, 16 * (c // 2 + 1))
                        t.wait_ge(s_sb.h, c + 1)
                        if c >= 2:
                            t.wait_ge(s_oe.h, c - 1)
                        for h in range(H):
                            t.wait_ge(s_m.h, 2 * c + h // 8 + 1)
                            ins = t.matmul(
                                ps_out[r][64 * (h % 2):64 * (h % 2) + 64,
                                          128 * (h // 2):128 * (h // 2) + 128],
                                vts[r][:, 64 * h:64 * (h + 1)],
                                attb[:, 128 * h:128 * (h + 1)],
                                start=True, stop=not (sparts & 2),
                                skip_group_check=True)
                            if sparts & 2:
                                ins = t.matmul(
                                    ps_out[r][64 * (h % 2):64 * (h % 2) + 64,
                                              128 * (h // 2):128 * (h // 2) + 128],
                                    sbf[c % 2][64 * (h % 2):64 * (h % 2) + 64,
                                               64 * (h // 2):64 * (h // 2) + 64],
                                    qzs[r][64 * (h % 2):64 * (h % 2) + 64,
                                           128 * (h // 2):128 * (h // 2) + 128],
                                    start=False, stop=True, skip_group_check=True)
                        s_o.inc(ins)
                        t.wait_ge(s_kt[r].h, 16 * (c // 2 + 1))
                        if c >= 2:
                            t.wait_ge(s_acc.h, c - 1)
                        if sparts & 4:
                            for h in range(H):
                                ins = t.matmul(
                                    ps_S[r][64 * (h % 2):64 * (h % 2) + 64,
                                            64 * (h // 2):64 * (h // 2) + 64],
                                    kts[r][:, 64 * h:64 * (h + 1)],
                                    vts[r][:, 64 * h:64 * (h + 1)],
                                    start=True, stop=True, skip_group_check=True)
                        else:
                            ins = t.matmul(ps_S[r], kts[r][:, 0:128],
                                           vts[r][:, 0:512],
                                           start=True, stop=True,
                                           skip_group_check=True)
                        s_d.inc(ins)

                @blk.vector
                def _(v):
                    for c in range(NT):
                        r = c % 2
                        for half in range(2):
                            G = 2 * c + half
                            v.wait_ge(s_a.h, G + 1)
                            for j in range(4):
                                for par in range(2):
                                    h = 8 * half + 2 * j + par
                                    ins = v.tensor_mul(
                                        attb[:, 128 * h:128 * (h + 1)],
                                        ps_att[par][:, 128 * j:128 * (j + 1)],
                                        maskT)
                            s_m.inc(ins)
                        v.wait_ge(s_d.h, c + 1)
                        v.wait_ge(s_sb.h, c + 1)
                        ins = v.tensor_add(Sf32, Sf32, ps_S[r])
                        s_acc.inc(ins)
                        v.wait_ge(s_acc.h, s_acc.v)
                        ins = v.tensor_copy(out=sbf[(c + 1) % 2], in_=Sf32)
                        s_sb.inc(ins)

                @blk.scalar
                def _(s):
                    for c in range(NT):
                        r = c % 2
                        s.wait_ge(s_o.h, c + 1)
                        for g in range(NF):
                            ins = s.activation(
                                out=stream[g][:, 128 * c:128 * (c + 1)],
                                in_=ps_out[r][:, 128 * g:128 * (g + 1)],
                                func=AF.Copy)
                        s_oe.inc(ins)
        nc.all_engine_barrier()

        if nph < 3:
            return
        s_x = [C.sem("xq0"), C.sem("xq1")]
        s_pw = C.sem("pw")
        s_r1 = C.sem("r1S")
        s_ro = [C.sem("ro0"), C.sem("ro1")]
        s_st2 = C.sem("st2")
        s_sd2 = C.sem("sd2")
        s_xn2 = C.sem("xn2S")
        s_tp2 = C.sem("tp2")
        s_te2 = C.sem("te2")
        s_xo = C.sem("xoS")
        chw = Chain(C, "chw")

        with ExitStack() as psc:
            P = Ctx(nc, psc, pref)
            ps_w = [P.ps(f"pw{i}", [128, 512], F32) for i in range(4)]
            ps_tpB = [P.ps(f"ptb{i}", [128, 1024], BF16) for i in range(2)]

            with nc.Block() as blk:
                @blk.sync
                def _(sy):
                    for q in range(4):
                        if q >= 1:
                            sy.wait_ge(s_xn2.h, q)
                            if write_r1:
                                sy.wait_ge(s_ro[(q - 1) % 2].h,
                                           64 * ((q - 1) // 2 + 1))
                        for j in range(4):
                            r0 = 512 * q + 128 * j
                            sy.dma_start(out=xq[:, 1024 * j:1024 * (j + 1)],
                                         in_=xs[r0:r0 + 128, :]
                                         ).then_inc(s_x[q % 2].h, 16)
                            s_x[q % 2].v += 16

                @blk.tensor
                def _(t):
                    t.wait_ge(s_w.h, 48)
                    gi = 0

                    def transposes(t, q):
                        t.wait_ge(s_xn2.h, q + 1)
                        for j in range(4):
                            i = 4 * q + j
                            if i >= 2:
                                t.wait_ge(s_te2.h, i - 1)
                            for k in range(NF):
                                ins = t.transpose(
                                    ps_tpB[i % 2][:, 128 * k:128 * (k + 1)],
                                    xn2q[:, 1024 * j + 128 * k:
                                         1024 * j + 128 * (k + 1)],
                                    ident)
                            s_tp2.inc(ins)

                    for q in range(4):
                        for i in range(4 * q, 4 * q + 4):
                            for n in range(2):
                                if gi >= 4:
                                    t.wait_ge(s_r1.h, gi - 3)
                                for k in range(NF):
                                    t.matmul(
                                        ps_w[gi % 4],
                                        stream[k][:, 128 * i:128 * (i + 1)],
                                        wo_b[:, 1024 * k + 512 * n:
                                             1024 * k + 512 * (n + 1)],
                                        start=(k == 0), stop=False)
                                ins = t.matmul(ps_w[gi % 4], onesr,
                                               bo_b[:, 512 * n:512 * (n + 1)],
                                               start=False, stop=True)
                                s_pw.inc(ins)
                                gi += 1
                        if q >= 1:
                            transposes(t, q - 1)
                    transposes(t, 3)

                @blk.vector
                def _(v):
                    for q in range(4):
                        v.wait_ge(s_x[q % 2].h, 64 * (q // 2 + 1))
                        for j in range(4):
                            for n in range(2):
                                gi = 2 * (4 * q + j) + n
                                v.wait_ge(s_pw.h, gi + 1)
                                ins = v.scalar_tensor_tensor(
                                    xq[:, 1024 * j + 512 * n:
                                       1024 * j + 512 * (n + 1)],
                                    ps_w[gi % 4], 1.0,
                                    xq[:, 1024 * j + 512 * n:
                                       1024 * j + 512 * (n + 1)],
                                    ALU.mult, ALU.add)
                                s_r1.inc(ins)
                        v.wait_ge(s_r1.h, s_r1.v)
                        for j in range(4):
                            ins = v.reduce_sum(stat[:, 4 * q + j:4 * q + j + 1],
                                               xq[:, 1024 * j:1024 * (j + 1)],
                                               axis=AXX)
                        chw.bar(v, ins)
                        for j in range(4):
                            ins = v.tensor_scalar(
                                xcq[:, 1024 * j:1024 * (j + 1)],
                                xq[:, 1024 * j:1024 * (j + 1)],
                                1024.0, stat[:, 4 * q + j:4 * q + j + 1],
                                ALU.mult, ALU.subtract)
                        chw.bar(v, ins)
                        for j in range(4):
                            ins = v.tensor_mul(xcq[:, 1024 * j:1024 * (j + 1)],
                                               xcq[:, 1024 * j:1024 * (j + 1)],
                                               xcq[:, 1024 * j:1024 * (j + 1)])
                        chw.bar(v, ins)
                        for j in range(4):
                            ins = v.reduce_sum(
                                stat[:, 16 + 4 * q + j:16 + 4 * q + j + 1],
                                xcq[:, 1024 * j:1024 * (j + 1)], axis=AXX)
                        s_st2.inc(ins)
                        v.wait_ge(s_st2.h, s_st2.v)
                        for j in range(4):
                            ins = v.tensor_scalar(
                                xcq[:, 1024 * j:1024 * (j + 1)],
                                xq[:, 1024 * j:1024 * (j + 1)],
                                1024.0, stat[:, 4 * q + j:4 * q + j + 1],
                                ALU.mult, ALU.subtract)
                        chw.bar(v, ins)
                        v.wait_ge(s_sd2.h, q + 1)
                        ins = v.reciprocal(stat[:, 48 + 4 * q:48 + 4 * q + 4],
                                           stat[:, 32 + 4 * q:32 + 4 * q + 4])
                        chw.bar(v, ins)
                        if q >= 1:
                            v.wait_ge(s_tp2.h, 4 * q)
                            if write_xn2:
                                v.wait_ge(s_xo.h, 64 * q)
                        for j in range(4):
                            ins = v.tensor_scalar(
                                xn2q[:, 1024 * j:1024 * (j + 1)],
                                xcq[:, 1024 * j:1024 * (j + 1)],
                                stat[:, 48 + 4 * q + j:48 + 4 * q + j + 1],
                                1.0 / 1024.0, ALU.mult, ALU.mult)
                        s_xn2.inc(ins)

                @blk.scalar
                def _(s):
                    def evacs(s, q):
                        for j in range(4):
                            i = 4 * q + j
                            s.wait_ge(s_tp2.h, i + 1)
                            for k in range(NF):
                                ins = s.activation(
                                    out=xn2_fm[k][:, 128 * i:128 * (i + 1)],
                                    in_=ps_tpB[i % 2][:, 128 * k:128 * (k + 1)],
                                    func=AF.Copy)
                            s_te2.inc(ins)
                    for q in range(4):
                        s.wait_ge(s_st2.h, q + 1)
                        if q == 0:
                            s.wait_ge(s_cns.h, 1)
                        s_sd2.inc(s.activation(
                            out=stat[:, 32 + 4 * q:32 + 4 * q + 4],
                            in_=stat[:, 16 + 4 * q:16 + 4 * q + 4],
                            func=AF.Sqrt, bias=cns[:, 0:1], scale=cns[:, 1:2]))
                        if q >= 1:
                            evacs(s, q - 1)
                    evacs(s, 3)

                @blk.gpsimd
                def _(g):
                    for q in range(4):
                        for j in range(4):
                            i = 4 * q + j
                            g.wait_ge(s_r1.h, 2 * (i + 1))
                            dst = r1_dst(i, g) if write_r1 else None
                            if dst is not None:
                                g.dma_start(out=dst,
                                            in_=xq[:, 1024 * j:1024 * (j + 1)]
                                            ).then_inc(s_ro[q % 2].h, 16)
                                s_ro[q % 2].v += 16
                        g.wait_ge(s_xn2.h, q + 1)
                        for j in range(4):
                            i = 4 * q + j
                            if write_xn2:
                                xn2_wr(g, i,
                                       xn2q[:, 1024 * j:1024 * (j + 1)]
                                       ).then_inc(s_xo.h, 16)
                                s_xo.v += 16
                    g.wait_ge(s_xo.h, s_xo.v)
                    g.wait_ge(s_ro[0].h, s_ro[0].v)
                    g.wait_ge(s_ro[1].h, s_ro[1].v)
        nc.all_engine_barrier()

        if nph < 4:
            return
        s_g1 = C.sem("g1")
        s_g2 = C.sem("g2")
        s_g3 = C.sem("g3")
        s_g4 = C.sem("g4")
        s_g5 = C.sem("g5")
        s_po = C.sem("poS")
        s_pd = C.sem("pdS")

        with ExitStack() as psc:
            P = Ctx(nc, psc, pref)
            ps_ga = P.ps("psga", [128, 512], F32)
            ps_gb = P.ps("psgb", [128, 512], F32)
            ps_gc = P.ps("psgc", [128, 512], F32)

            with nc.Block() as blk:
                @blk.tensor
                def _(t):
                    for n in range(4):
                        if n >= 1:
                            t.wait_ge(s_g2.h, n)
                        for k in range(NF):
                            ins = t.matmul(
                                ps_ga[0:NE, :],
                                gw_b[:, NE * k:NE * (k + 1)],
                                xn2_fm[k][:, 512 * n:512 * (n + 1)],
                                start=(k == 0), stop=(k == NF - 1))
                        s_g1.inc(ins)
                    for n in range(4):
                        t.wait_ge(s_g2.h, n + 1)
                        if n >= 1:
                            t.wait_ge(s_g3.h, n)
                        ins = t.matmul(ps_gb[0:1, :], ones8,
                                       esc[0:NE, 512 * n:512 * (n + 1)],
                                       start=True, stop=True,
                                       skip_group_check=True)
                        s_g4.inc(ins)
                    for n in range(4):
                        t.wait_ge(s_g3.h, n + 1)
                        if n >= 1:
                            t.wait_ge(s_g5.h, n)
                        ins = t.matmul(ps_gc[0:NE, :], ones18,
                                       rec[0:1, 512 * n:512 * (n + 1)],
                                       start=True, stop=True,
                                       skip_group_check=True)
                        s_po.inc(ins)

                @blk.scalar
                def _(s):
                    for n in range(4):
                        s.wait_ge(s_g1.h, n + 1)
                        s_g2.inc(s.activation(out=esc[0:NE, 512 * n:512 * (n + 1)],
                                              in_=ps_ga[0:NE, :], func=AF.Exp))

                @blk.vector
                def _(v):
                    for n in range(4):
                        v.wait_ge(s_g4.h, n + 1)
                        ins = v.reciprocal(rec[0:1, 512 * n:512 * (n + 1)],
                                           ps_gb[0:1, :])
                        s_g3.inc(ins)
                        v.wait_ge(s_g3.h, s_g3.v)
                    for n in range(4):
                        v.wait_ge(s_po.h, n + 1)
                        ins = v.tensor_mul(prb[0:NE, 512 * n:512 * (n + 1)],
                                           esc[0:NE, 512 * n:512 * (n + 1)],
                                           ps_gc[0:NE, :])
                        s_g5.inc(ins)

                @blk.gpsimd
                def _(g):
                    g.wait_ge(s_g5.h, 4)
                    probs_wr(g, prb[0:NE, :]).then_inc(s_pd.h, 16)
                    s_pd.v += 16
                    g.wait_ge(s_pd.h, s_pd.v)
        nc.all_engine_barrier()


def build_L2(nph=4, sparts=7):
    nc = bass.Bass("TRN2", target_bir_lowering=False, debug=False)
    xs = nc.dram_tensor("xs", [SL, D], F32, kind="ExternalInput").ap()
    qz_d = nc.dram_tensor("qz", [D, SL], BF16, kind="ExternalInput").ap()
    kf_d = nc.dram_tensor("kf", [D, SL], BF16, kind="ExternalInput").ap()
    ktm_d = nc.dram_tensor("ktm", [SL, D], BF16, kind="ExternalInput").ap()
    vtm_d = nc.dram_tensor("vtm", [SL, D], BF16, kind="ExternalInput").ap()
    sall_d = nc.dram_tensor("sall", [NCORE, 128, 512], F32, kind="ExternalInput").ap()
    msk_d = nc.dram_tensor("msk", [128, NCORE], F32, kind="ExternalInput").ap()
    wo_d = nc.dram_tensor("wo", [D, D], F32, kind="ExternalInput").ap()
    bo_d = nc.dram_tensor("bo", [1, D], F32, kind="ExternalInput").ap()
    gw_d = nc.dram_tensor("gw", [D, NE], F32, kind="ExternalInput").ap()
    r1_o = nc.dram_tensor("r1", [SL, D], F32, kind="ExternalOutput").ap()
    xn2_o = nc.dram_tensor("xn2", [SL, D], BF16, kind="ExternalOutput").ap()
    probs_o = nc.dram_tensor("probs", [NE, SL], F32, kind="ExternalOutput").ap()
    ident_np, maskT_np, _, _ = consts_np()
    ident_c = nc.inline_tensor(ident_np, name="identC").ap()
    maskT_c = nc.inline_tensor(maskT_np, name="maskTC").ap()
    ones_c = nc.inline_tensor(np.ones((1, 128), dtype=BFNP), name="onesC").ap()
    ones8f_c = nc.inline_tensor(np.ones((8, 1), dtype=np.float32), name="o8C").ap()
    ones18f_c = nc.inline_tensor(np.ones((1, 8), dtype=np.float32), name="o18C").ap()
    emit_L2(nc, xs, qz_d, kf_d, ktm_d, vtm_d, sall_d, msk_d, wo_d, bo_d, gw_d,
            r1_o, xn2_o, probs_o, ident_c, maskT_c, ones_c, ones8f_c,
            ones18f_c, NSRC=NCORE, nph=nph, sparts=sparts)
    return nc



def build_L3():
    nc = bass.Bass("TRN2", target_bir_lowering=False, debug=False)
    xn2f_d = nc.dram_tensor("xn2f", [T, D], BF16, kind="ExternalInput").ap()
    prow_d = nc.dram_tensor("prow", [1, T], F32, kind="ExternalInput").ap()
    w1_d = nc.dram_tensor("w1", [D, HID], F32, kind="ExternalInput").ap()
    w2_d = nc.dram_tensor("w2", [HID, D], F32, kind="ExternalInput").ap()

    Y_o = nc.dram_tensor("Y", [CAP, D], F32, kind="ExternalOutput").ap()
    slot_o = nc.dram_tensor("slot", [1, T], F32, kind="ExternalOutput").ap()

    ident_np, _, _, _ = consts_np()
    ident_c = nc.inline_tensor(ident_np, name="identC").ap()
    lt64_c = nc.inline_tensor(
        np.triu(np.ones((64, 64), np.float32), 1), name="lt64C").ap()
    o64sq_c = nc.inline_tensor(np.ones((64, 64), np.float32), name="o64C").ap()
    o64_c = nc.inline_tensor(np.ones((64, 1), np.float32), name="o641C").ap()

    st_dram = nc.dram_tensor("stD", [64, 128], F32, kind="Internal").ap()
    w2bf = nc.dram_tensor("w2bf", [HID, D], BF16, kind="Internal").ap()

    NITER = 40

    with ExitStack() as es:
        C = Ctx(nc, es)
        w1_b = C.sb("w1b", [128, NF * HID], BF16)
        w2r = [C.sb(f"w2r{i}", [128, 1024], BF16) for i in range(2)]
        xin = [C.sb(f"xin{i}", [128, 1024], BF16) for i in range(8)]
        xinT = [C.sb(f"xit{k}", [128, 1024], BF16) for k in range(NF)]
        hfm = [C.sb(f"hf{m}", [128, 1024], BF16) for m in range(32)]
        ybuf = [C.sb(f"yb{i}", [128, 1024], F32) for i in range(2)]
        pr = C.sb("prS", [64, 128], F32)
        th = C.sb("thS", [64, 1], F32)
        thm = C.sb("thmS", [64, 1], F32)
        bcol = C.sb("bcolS", [64, 1], F32)
        c64 = C.sb("c64S", [64, 1], F32)
        cnt64 = C.sb("cntS", [64, 1], F32)
        cmp64 = C.sb("cmp64", [64, 128], F32)
        m64 = C.sb("m64", [64, 128], F32)
        s64 = C.sb("s64", [64, 128], F32)
        z64 = C.sb("z64", [64, 128], F32)
        off64 = C.sb("off64", [64, 1], F32)
        sofA = C.sb("sofA", [64, 128], F32)
        indA = C.sb("indA", [64, 2048], F32)
        srowA = C.sb("srowA", [128, 2048], F32)
        iotaR = C.sb("iotaR", [64, 128], F32)
        iotaC = C.sb("iotaC", [128, 1], F32)
        sek = C.sb("sek", [64, 16], F32)
        nbf = C.sb("nbf", [128, 16], F32)
        nbi = C.sb("nbi", [128, 16], I32)
        scol = C.sb("scol", [128, 16], F32)
        wcmp = C.sb("wcmp", [128, 128], F32)
        wcol = C.sb("wcol", [128, 16], F32)
        idxf = C.sb("idxf", [128, 16], F32)
        idxi = C.sb("idxi", [128, 16], I32)
        lt64 = C.sb("lt64S", [64, 64], F32)
        o64sq = C.sb("o64sqS", [64, 64], F32)
        o64 = C.sb("o64S", [64, 1], F32)
        ident = C.sb("identS", [128, 128], BF16)

        s_c = C.sem("consts")
        s_w1 = C.sem("w1S")
        s_io = C.sem("ioS")
        s_w2p = [C.sem("w2p0"), C.sem("w2p1")]
        s_w2s = [C.sem("w2s0"), C.sem("w2s1")]
        chv = Chain(C, "chvL3")

        with nc.Block() as blk:
            @blk.sync
            def _(sy):
                sy.dma_start(out=ident, in_=ident_c).then_inc(s_c.h, 16)
                sy.dma_start(out=lt64, in_=lt64_c).then_inc(s_c.h, 16)
                sy.dma_start(out=o64sq, in_=o64sq_c).then_inc(s_c.h, 16)
                sy.dma_start(out=o64, in_=o64_c).then_inc(s_c.h, 16)
                sy.dma_start(out=pr,
                             in_=prow_d[0].rearrange("(k j) -> k j", j=128)
                             ).then_inc(s_c.h, 16)
                s_c.v += 80

            @blk.gpsimd
            def _(g):
                g.dma_start(out=w1_b.rearrange("p (k m) -> p k m", m=HID),
                            in_=w1_d.rearrange("(k p) m -> p k m", p=128)
                            ).then_inc(s_w1.h, 16)
                s_w1.v += 16
                g.iota(iotaR, [[1, 128]], channel_multiplier=0,
                       allow_small_or_imprecise_dtypes=True)
                g.iota(iotaC, [[0, 1]], channel_multiplier=1,
                       allow_small_or_imprecise_dtypes=True)
                g.memset(z64, 0.0)
                g.memset(th, 0.5).then_inc(s_io.h, 1)
                s_io.v += 1
                for kc in range(HID // 128):
                    r = kc % 2
                    if kc >= 2:
                        g.wait_ge(s_w2s[r].h, 16 * (kc // 2))
                    g.dma_start(out=w2r[r],
                                in_=w2_d[128 * kc:128 * (kc + 1), :]
                                ).then_inc(s_w2p[r].h, 16)
                    s_w2p[r].v += 16
                    g.wait_ge(s_w2p[r].h, s_w2p[r].v)
                    g.dma_start(out=w2bf[128 * kc:128 * (kc + 1), :],
                                in_=w2r[r]).then_inc(s_w2s[r].h, 16)
                    s_w2s[r].v += 16
                g.wait_ge(s_w2s[0].h, s_w2s[0].v)
                g.wait_ge(s_w2s[1].h, s_w2s[1].v)
                g.wait_ge(s_w1.h, s_w1.v)
        nc.all_engine_barrier()

        with ExitStack() as psc:
            P = Ctx(nc, psc)
            s_cd = P.sem("cdS")
            s_bi = P.sem("biS")
            ps_c = P.ps("pcS", [64, 1], F32)

            with nc.Block() as blk:
                @blk.vector
                def _(v):
                    v.wait_ge(s_io.h, 1)
                    v.wait_ge(s_c.h, 80)
                    for it in range(NITER):
                        step = 0.25 * (0.5 ** it)
                        ins = v.tensor_scalar(cmp64, pr, th[:, 0:1], None,
                                              ALU.is_gt)
                        chv.bar(v, ins)
                        ins = v.reduce_sum(c64, cmp64, axis=AXX)
                        s_cd.inc(ins)
                        v.wait_ge(s_bi.h, it + 1)
                        ins = v.tensor_copy(out=cnt64, in_=ps_c)
                        chv.bar(v, ins)
                        ins = v.tensor_scalar(bcol, cnt64, float(CAP), None,
                                              ALU.is_gt)
                        chv.bar(v, ins)
                        ins = v.tensor_scalar(thm, th, step, None,
                                              ALU.subtract)
                        chv.bar(v, ins)
                        ins = v.scalar_tensor_tensor(th, bcol, 2.0 * step, thm,
                                                     ALU.mult, ALU.add)
                        chv.bar(v, ins)

                @blk.tensor
                def _(t):
                    for it in range(NITER):
                        t.wait_ge(s_cd.h, it + 1)
                        s_bi.inc(t.matmul(ps_c, o64sq, c64,
                                          start=True, stop=True,
                                          skip_group_check=True))
        nc.all_engine_barrier()

        with ExitStack() as psc:
            P = Ctx(nc, psc)
            s_sv = P.sem("svS")
            s_so = P.sem("soS")
            s_l2 = P.sem("l2S")
            s_sg = P.sem("sgS")
            s_od = P.sem("odS")
            s_stw = P.sem("stwS")
            ps_off = P.ps("poffS", [64, 1], F32)
            ps_l1 = P.ps("pl1S", [128, 16], F32)

            with nc.Block() as blk:
                @blk.vector
                def _(v):
                    ins = v.tensor_scalar(m64, pr, th[:, 0:1], None, ALU.is_gt)
                    chv.bar(v, ins)
                    ins = v.tensor_tensor_scan(s64, m64, z64, 0.0,
                                               ALU.add, ALU.add)
                    s_sv.inc(ins)
                    v.wait_ge(s_so.h, 1)
                    ins = v.tensor_copy(out=off64, in_=ps_off)
                    chv.bar(v, ins)
                    ins = v.tensor_scalar(s64, s64, off64[:, 0:1], None,
                                          ALU.add)
                    s_sv.inc(ins)
                    v.wait_ge(s_sv.h, s_sv.v)
                    ins = v.tensor_scalar(sofA, s64, 1.0 + BIGIDX, None,
                                          ALU.subtract)
                    chv.bar(v, ins)
                    ins = v.tensor_mul(sofA, sofA, m64)
                    chv.bar(v, ins)
                    ins = v.tensor_scalar(sofA, sofA, BIGIDX, None, ALU.add)
                    s_sv.inc(ins)
                    for k in range(16):
                        ins = v.tensor_scalar(sek[:, k:k + 1], s64[:, 127:128],
                                              float(128 * k), None,
                                              ALU.subtract)
                        chv.bar(v, ins)
                        ins = v.tensor_scalar(
                            indA[:, 128 * k:128 * (k + 1)], iotaR,
                            sek[:, k:k + 1], None, ALU.is_ge)
                        if k < 15:
                            chv.bar(v, ins)
                    s_sv.inc(ins)
                    v.wait_ge(s_l2.h, 1)
                    ins = v.tensor_copy(out=nbf, in_=ps_l1)
                    chv.bar(v, ins)
                    ins = v.tensor_copy(out=nbi, in_=nbf)
                    s_sv.inc(ins)
                    v.wait_ge(s_sg.h, 16 * 16)
                    for k in range(16):
                        ins = v.tensor_scalar(scol[:, k:k + 1], iotaC,
                                              float(128 * k), None, ALU.add)
                        chv.bar(v, ins)
                        ins = v.tensor_scalar(
                            wcmp, srowA[:, 128 * k:128 * (k + 1)],
                            scol[:, k:k + 1], None, ALU.is_le)
                        chv.bar(v, ins)
                        ins = v.reduce_sum(wcol[:, k:k + 1], wcmp, axis=AXX)
                        chv.bar(v, ins)
                        ins = v.scalar_tensor_tensor(
                            idxf[:, k:k + 1], nbf[:, k:k + 1], 128.0,
                            wcol[:, k:k + 1], ALU.mult, ALU.add)
                        chv.bar(v, ins)
                    ins = v.tensor_copy(out=idxi, in_=idxf)
                    s_sv.inc(ins)

                @blk.tensor
                def _(t):
                    t.wait_ge(s_sv.h, 1)
                    s_so.inc(t.matmul(ps_off, lt64, s64[:, 127:128],
                                      start=True, stop=True,
                                      skip_group_check=True))
                    t.wait_ge(s_sv.h, 4)
                    for k in range(16):
                        ins = t.matmul(ps_l1[:, k:k + 1],
                                       indA[:, 128 * k:128 * (k + 1)],
                                       o64, start=True, stop=True,
                                       skip_group_check=True)
                    s_l2.inc(ins)

                @blk.sync
                def _(sy):
                    sy.wait_ge(s_sv.h, 2)
                    sy.dma_start(out=st_dram, in_=s64).then_inc(s_stw.h, 16)
                    s_stw.v += 16
                    sy.wait_ge(s_sv.h, 3)
                    sy.dma_start(out=slot_o[0].rearrange("(k j) -> k j", j=128),
                                 in_=sofA).then_inc(s_od.h, 16)
                    s_od.v += 16
                    sy.wait_ge(s_od.h, s_od.v)
                    sy.wait_ge(s_stw.h, s_stw.v)

                @blk.gpsimd
                def _(g):
                    g.wait_ge(s_sv.h, 5)
                    g.wait_ge(s_stw.h, 16)
                    for k in range(16):
                        g.indirect_dma_start(
                            srowA[:, 128 * k:128 * (k + 1)], None,
                            st_dram, IndirectOffsetOnAxis(nbi[:, k:k + 1], 0),
                            bounds_check=63, oob_is_err=False,
                        ).then_inc(s_sg.h, 16)
                        s_sg.v += 16
                    g.wait_ge(s_sg.h, s_sg.v)
        nc.all_engine_barrier()

        for hs in range(2):
            with ExitStack() as psc:
                P = Ctx(nc, psc)
                s_xg = [P.sem(f"xg{hs}{j}") for j in range(8)]
                s_tp = P.sem(f"tp{hs}")
                s_te = P.sem(f"te{hs}")
                s_f = P.sem(f"f{hs}")
                s_fe = P.sem(f"fe{hs}")
                ps_tp = P.ps(f"ptp{hs}", [128, 1024], BF16)
                ps_f = [P.ps(f"pf{hs}{i}", [128, 512], F32) for i in range(3)]

                with nc.Block() as blk:
                    @blk.gpsimd
                    def _(g):
                        for j in range(8):
                            g.indirect_dma_start(
                                xin[j], None, xn2f_d,
                                IndirectOffsetOnAxis(
                                    idxi[:, 8 * hs + j:8 * hs + j + 1], 0),
                                bounds_check=T - 1, oob_is_err=False,
                            ).then_inc(s_xg[j].h, 16)
                            s_xg[j].v += 16

                    @blk.tensor
                    def _(t):
                        for j in range(8):
                            t.wait_ge(s_xg[j].h, 16)
                            for k in range(NF):
                                ins = t.transpose(
                                    ps_tp[:, 128 * k:128 * (k + 1)],
                                    xin[j][:, 128 * k:128 * (k + 1)], ident)
                            s_tp.inc(ins)
                            t.wait_ge(s_te.h, j + 1)
                        gi = 0
                        for m in range(32):
                            for n in range(2):
                                if gi >= 3:
                                    t.wait_ge(s_fe.h, gi - 2)
                                for k in range(NF):
                                    ins = t.matmul(
                                        ps_f[gi % 3],
                                        w1_b[:, HID * k + 128 * m:
                                             HID * k + 128 * (m + 1)],
                                        xinT[k][:, 512 * n:512 * (n + 1)],
                                        start=(k == 0), stop=(k == NF - 1))
                                s_f.inc(ins)
                                gi += 1

                    @blk.scalar
                    def _(s):
                        for j in range(8):
                            s.wait_ge(s_tp.h, j + 1)
                            for k in range(NF):
                                ins = s.activation(
                                    out=xinT[k][:, 128 * j:128 * (j + 1)],
                                    in_=ps_tp[:, 128 * k:128 * (k + 1)],
                                    func=AF.Copy)
                            s_te.inc(ins)
                        gi = 0
                        for m in range(32):
                            for n in range(2):
                                s.wait_ge(s_f.h, gi + 1)
                                s_fe.inc(s.activation(
                                    out=hfm[m][:, 512 * n:512 * (n + 1)],
                                    in_=ps_f[gi % 3], func=AF.Gelu))
                                gi += 1
            nc.all_engine_barrier()

            for ph in range(2):
                with ExitStack() as psc:
                    P = Ctx(nc, psc)
                    s_wl = [P.sem(f"wl{hs}{ph}0"), P.sem(f"wl{hs}{ph}1")]
                    s_y = P.sem(f"y{hs}{ph}")
                    s_ye = P.sem(f"ye{hs}{ph}")
                    s_yo = [P.sem(f"yo{hs}{ph}0"), P.sem(f"yo{hs}{ph}1")]
                    ps_y = [P.ps(f"py{hs}{ph}{i}", [128, 1024], F32)
                            for i in range(4)]

                    with nc.Block() as blk:
                        @blk.sync
                        def _(sy):
                            for kc in range(32):
                                r = kc % 2
                                if kc >= 2:
                                    sy.wait_ge(s_y.h, kc - 1)
                                sy.dma_start(
                                    out=w2r[r],
                                    in_=w2bf[128 * kc:128 * (kc + 1), :]
                                    ).then_inc(s_wl[r].h, 16)
                                s_wl[r].v += 16

                        @blk.tensor
                        def _(t):
                            for kc in range(32):
                                r = kc % 2
                                t.wait_ge(s_wl[r].h, 16 * (kc // 2 + 1))
                                for mt in range(4):
                                    for n in range(2):
                                        ins = t.matmul(
                                            ps_y[mt][:, 512 * n:512 * (n + 1)],
                                            hfm[kc][:, 128 * (4 * ph + mt):
                                                    128 * (4 * ph + mt + 1)],
                                            w2r[r][:, 512 * n:512 * (n + 1)],
                                            start=(kc == 0), stop=(kc == 31))
                                s_y.inc(ins)

                        @blk.vector
                        def _(v):
                            v.wait_ge(s_y.h, 32)
                            for mt in range(4):
                                if mt >= 2:
                                    v.wait_ge(s_yo[mt % 2].h, 16)
                                ins = v.tensor_copy(out=ybuf[mt % 2],
                                                    in_=ps_y[mt])
                                s_ye.inc(ins)

                        @blk.gpsimd
                        def _(g):
                            for mt in range(4):
                                g.wait_ge(s_ye.h, mt + 1)
                                i8 = 8 * hs + 4 * ph + mt
                                g.dma_start(
                                    out=Y_o[128 * i8:128 * (i8 + 1), :],
                                    in_=ybuf[mt % 2]).then_inc(
                                        s_yo[mt % 2].h, 16)
                                s_yo[mt % 2].v += 16
                            g.wait_ge(s_yo[0].h, s_yo[0].v)
                            g.wait_ge(s_yo[1].h, s_yo[1].v)
                nc.all_engine_barrier()
    return nc



def build_L4():
    nc = bass.Bass("TRN2", target_bir_lowering=False, debug=False)
    acc_d = nc.dram_tensor("acc", [SL, D], F32, kind="ExternalInput").ap()
    Y_d = [nc.dram_tensor(f"Ye{e}", [CAP, D], F32, kind="ExternalInput").ap()
           for e in range(2)]
    slot_d = nc.dram_tensor("slotp", [2, SL], F32, kind="ExternalInput").ap()
    prb_d = nc.dram_tensor("prb2", [2, SL], F32, kind="ExternalInput").ap()

    out_o = nc.dram_tensor("accout", [SL, D], F32, kind="ExternalOutput").ap()

    with ExitStack() as es:
        C = Ctx(nc, es)
        acc = [C.sb(f"ac{i}", [128, D], F32) for i in range(NT)]
        scr = [C.sb(f"scr{i}", [128, D], F32) for i in range(2)]
        slt = C.sb("slt", [128, 2 * NT], F32)
        slti = C.sb("slti", [128, 2 * NT], I32)
        pmv = C.sb("pmv", [128, 2 * NT], F32)
        msk = C.sb("mskL", [128, 2 * NT], F32)

        s_al = [C.sem(f"al{i}") for i in range(NT)]
        s_sl = C.sem("slS")
        s_pm = C.sem("pmS")
        s_g = [C.sem(f"gS{n}") for n in range(2 * NT)]
        s_ms = C.sem("msS")
        s_ac = C.sem("acS")
        s_oo = [C.sem("oo0"), C.sem("oo1")]
        chv = Chain(C, "chvL4")

        with nc.Block() as blk:
            @blk.sync
            def _(sy):
                for i in range(NT):
                    sy.dma_start(out=acc[i], in_=acc_d[128 * i:128 * (i + 1), :]
                                 ).then_inc(s_al[i].h, 16)
                    s_al[i].v += 16
                for e in range(2):
                    for i in range(NT):
                        sy.dma_start(out=slt[:, NT * e + i:NT * e + i + 1],
                                     in_=slot_d[e, 128 * i:128 * (i + 1)]
                                     .rearrange("(p o) -> p o", o=1)
                                     ).then_inc(s_sl.h, 16)
                        s_sl.v += 16
                        sy.dma_start(out=pmv[:, NT * e + i:NT * e + i + 1],
                                     in_=prb_d[e, 128 * i:128 * (i + 1)]
                                     .rearrange("(p o) -> p o", o=1)
                                     ).then_inc(s_sl.h, 16)
                        s_sl.v += 16

            @blk.vector
            def _(v):
                v.wait_ge(s_sl.h, s_sl.v)
                ins = v.tensor_copy(out=slti, in_=slt)
                chv.bar(v, ins)
                ins = v.tensor_scalar(msk, slt, float(CAP), None, ALU.is_lt)
                chv.bar(v, ins)
                ins = v.tensor_mul(pmv, pmv, msk)
                s_pm.inc(ins)
                n = 0
                for e in range(2):
                    for i in range(NT):
                        v.wait_ge(s_g[n].h, 16)
                        if e == 0:
                            v.wait_ge(s_al[i].h, 16)
                        col = NT * e + i
                        ins = v.scalar_tensor_tensor(
                            acc[i], scr[n % 2], pmv[:, col:col + 1],
                            acc[i], ALU.mult, ALU.add)
                        s_ac.inc(ins)
                        v.wait_ge(s_ac.h, s_ac.v)
                        n += 1

            @blk.gpsimd
            def _(g):
                g.memset(scr[0], 0.0)
                g.memset(scr[1], 0.0).then_inc(s_ms.h, 1)
                g.wait_ge(s_ms.h, 1)
                g.wait_ge(s_pm.h, 1)
                n = 0
                for e in range(2):
                    for i in range(NT):
                        if n >= 2:
                            g.wait_ge(s_ac.h, n - 1)
                        col = NT * e + i
                        g.indirect_dma_start(
                            scr[n % 2], None, Y_d[e],
                            IndirectOffsetOnAxis(slti[:, col:col + 1], 0),
                            bounds_check=CAP - 1, oob_is_err=False,
                        ).then_inc(s_g[n].h, 16)
                        s_g[n].v += 16
                        n += 1

            @blk.scalar
            def _(s):
                for i in range(NT):
                    s.wait_ge(s_ac.h, NT + i + 1)
                    s.dma_start(out=out_o[128 * i:128 * (i + 1), :],
                                in_=acc[i]).then_inc(s_oo[i % 2].h, 16)
                    s_oo[i % 2].v += 16
                s.wait_ge(s_oo[0].h, s_oo[0].v)
                s.wait_ge(s_oo[1].h, s_oo[1].v)
    return nc






ECAP = 2048
NST = ECAP // 128
NITER_BIS = 40


def fused_consts_np():
    triu16 = np.triu(np.ones((16, 16), np.float32), 1)
    bdtriu16 = np.kron(np.eye(8, dtype=np.float32), triu16)
    qlt = np.zeros((64, 64), np.float32)
    for k in range(64):
        for m in range(64):
            if k // 16 < m // 16:
                qlt[k, m] = 1.0
    mglob = bdtriu16 + np.kron(np.eye(2, dtype=np.float32), qlt)
    bd64 = np.kron(np.eye(2, dtype=np.float32), np.ones((64, 64), np.float32))
    iotaF = np.tile(np.arange(128, dtype=np.float32), (128, 1))
    iotaC = np.arange(128, dtype=np.float32).reshape(128, 1)
    cpb = np.zeros((128, 2), np.float32)
    for p in range(128):
        q, jj, k = (p % 64) // 16, p // 64, p % 16
        for b in range(2):
            cpb[p, b] = 128 * q + 32 * b + 16 * jj + k
    return (np.ascontiguousarray(mglob), np.ascontiguousarray(bd64),
            np.ascontiguousarray(iotaF), np.ascontiguousarray(iotaC),
            np.ascontiguousarray(cpb))


def host_core_layout(c, nonce):
    b, h, r = c // 4, (c // 2) % 2, c % 2
    q_own, q_other = 2 * h + r, 2 * (1 - h) + r
    hb = 2 * h
    meta = np.zeros(96, np.int32)
    meta[0], meta[1], meta[2], meta[3] = q_own, q_other, r, 1 - r
    meta[6], meta[7] = q_own * 8, q_other * 8
    for mt in range(16):
        meta[8 + mt] = r * SL + 128 * mt
    for j in range(4):
        meta[48 + j] = r * 4 + j
        meta[72 + j] = (r * 4 + j) * 64
    for i in range(16):
        meta[56 + i] = q_own * 16 + i
    for k in range(4):
        meta[80 + k] = nonce + 1 + k
    msks = np.zeros((2, 128, 4), np.float32)
    for sl, q in ((0, q_own), (1, q_other)):
        msks[sl, :, :q] = 1.0
    triu16 = np.triu(np.ones((16, 16), np.float32), 1)
    selh = np.kron(np.eye(8, dtype=np.float32), triu16)
    qh = np.zeros((64, 64), np.float32)
    qh[16 * hb:16 * hb + 16, 16 * (hb + 1):16 * (hb + 1) + 16] = 1.0
    selh += np.kron(np.eye(2, dtype=np.float32), qh)
    myh = np.zeros((128, 1), np.float32)
    for p in range(128):
        if (p % 64) // 16 in (hb, hb + 1):
            myh[p] = 1.0
    hbase = np.full((128, 1), 16.0 * hb, np.float32)
    return dict(b=b, h=h, r=r, q_own=q_own, q_other=q_other,
                meta=np.ascontiguousarray(meta.reshape(1, 96)),
                msks=np.ascontiguousarray(msks),
                selh=np.ascontiguousarray(selh.astype(np.float32)),
                myh=myh, hbase=hbase)


class FusedCtx:
    pass


def emit_pair_barrier(nc, C, F, bar):
    with nc.Block() as blk:
        @blk.gpsimd
        def _(g):
            gs = [g.engine]
            s_fw = C.sem(f"bfw{bar}")
            s_pl = C.sem(f"bpl{bar}")
            rfw = g.value_load(F.ifl[0:1, bar:bar + 1])
            g.dma_start(out=F.SH_FLf[bass.ds(rfw, 1), :],
                        in_=F.metaS[0:1, 80 + bar:82 + bar]
                        ).then_inc(s_fw.h, 16)
            g.wait_ge(s_fw.h, 16)
            rpr = g.value_load(F.ifl[0:1, 4 + bar:5 + bar])
            rtgt = g.value_load(F.metaS[0:1, 80 + bar:81 + bar])
            rc = nc.alloc_registers(f"brc{bar}", engines=gs)
            nc.regs_mov(rc, 0)
            nc.br(f"bpoll{bar}", engines=gs)
            with nc.body(f"bpoll{bar}", valid_engines=gs, nocache=True):
                nc.regs_alu(rc, rc, 16, op=ALU.add)
                g.dma_start(out=F.flagr,
                            in_=F.SH_FLf[bass.ds(rpr, 1), :]
                            ).then_inc(s_pl.h, 16)
                g.wait_ge(s_pl.h, rc)
                rv = g.value_load(F.flagr[0:1, 0:1])
                nc.br_ne(rv, rtgt, f"bpoll{bar}", f"bpend{bar}", engines=gs)
            nc.switch_bb(f"bpend{bar}")
    nc.all_engine_barrier()


def clear_free_sems(nc):
    from concourse.bass import get_kernel_semaphore_range
    nc.all_engine_barrier()
    nums = sorted(n for n in get_kernel_semaphore_range()
                  if nc.free_contains(n))
    i = 0
    while i < len(nums):
        jx = i
        while jx + 1 < len(nums) and nums[jx + 1] == nums[jx] + 1:
            jx += 1
        rg = range(nums[i], nums[jx] + 1)
        nc.gpsimd.dma_reset(rg)
        nc.gpsimd.sem_clear(rg)
        i = jx + 1
    nc.all_engine_barrier()


FUSED_STOP = "ALL"
FUSED_NPHC = 3
FUSED_SLPM = True


def build_fused():
    stop = FUSED_STOP
    nc = bass.Bass("TRN2", target_bir_lowering=False, debug=False)
    xq2 = nc.dram_tensor("xq2", [2, SL, D], F32, kind="ExternalInput").ap()
    wq_d = nc.dram_tensor("wq", [D, D], F32, kind="ExternalInput").ap()
    wk_d = nc.dram_tensor("wk", [D, D], F32, kind="ExternalInput").ap()
    wv_d = nc.dram_tensor("wv", [D, D], F32, kind="ExternalInput").ap()
    wo_d = nc.dram_tensor("wo", [D, D], F32, kind="ExternalInput").ap()
    bo_d = nc.dram_tensor("bo", [1, D], F32, kind="ExternalInput").ap()
    gw_d = nc.dram_tensor("gw", [D, NE], F32, kind="ExternalInput").ap()
    w1e_d = nc.dram_tensor("w1e", [4, D, HID], F32, kind="ExternalInput").ap()
    w2e_d = nc.dram_tensor("w2e", [4, HID, D], F32, kind="ExternalInput").ap()
    msks_d = nc.dram_tensor("msks", [2, 128, 4], F32,
                            kind="ExternalInput").ap()
    meta_d = nc.dram_tensor("meta", [1, 96], I32, kind="ExternalInput").ap()
    selh_d = nc.dram_tensor("selh", [128, 128], F32, kind="ExternalInput").ap()
    myh_d = nc.dram_tensor("myh", [128, 1], F32, kind="ExternalInput").ap()
    hbase_d = nc.dram_tensor("hbase", [128, 1], F32,
                             kind="ExternalInput").ap()

    out_o = nc.dram_tensor("out", [SL, D], F32, kind="ExternalOutput").ap()

    qzP = [nc.dram_tensor(f"qzP{s}", [D, SL], BF16, kind="Internal").ap()
           for s in range(2)]
    kfP = [nc.dram_tensor(f"kfP{s}", [D, SL], BF16, kind="Internal").ap()
           for s in range(2)]
    ktP = [nc.dram_tensor(f"ktP{s}", [SL, D], BF16, kind="Internal").ap()
           for s in range(2)]
    vtP = [nc.dram_tensor(f"vtP{s}", [SL, D], BF16, kind="Internal").ap()
           for s in range(2)]
    ssP = [nc.dram_tensor(f"ssP{s}", [128, 512], F32, kind="Internal").ap()
           for s in range(2)]
    r1P = nc.dram_tensor("r1P", [SL, D], F32, kind="Internal").ap()
    stP = nc.dram_tensor("stP", [256, 128], F32, kind="Internal").ap()

    SH_S = nc.dram_tensor("SH_S", [4, 128, 512], F32, kind="Internal",
                          addr_space="Shared").ap()
    SH_PR = nc.dram_tensor("SH_PR", [32, 16, 128], F32, kind="Internal",
                           addr_space="Shared").ap()
    SH_XN = nc.dram_tensor("SH_XN", [2 * SL, D], BF16, kind="Internal",
                           addr_space="Shared").ap()
    SH_Y = nc.dram_tensor("SH_Y", [NE * ECAP, D], F32, kind="Internal",
                          addr_space="Shared").ap()
    SH_SL = nc.dram_tensor("SH_SL", [NE * 64, 128], F32, kind="Internal",
                           addr_space="Shared").ap()
    SH_PM = nc.dram_tensor("SH_PM", [NE * 64, 128], F32, kind="Internal",
                           addr_space="Shared").ap()
    SH_FL = nc.dram_tensor("SH_FL", [32, 2], I32, kind="Internal",
                           addr_space="Shared").ap()

    ident_np, maskT_np, blockpat_np, E_np = consts_np()
    mglob_np, bd64_np, iotaF_np, iotaC_np, cpb_np = fused_consts_np()
    ident_c = nc.inline_tensor(ident_np, name="identC").ap()
    maskT_c = nc.inline_tensor(maskT_np, name="maskTC").ap()
    bp_c = nc.inline_tensor(blockpat_np, name="bpC").ap()
    E_c = nc.inline_tensor(E_np, name="EC").ap()
    ones_c = nc.inline_tensor(np.ones((1, 128), dtype=BFNP), name="onesC").ap()
    ones8f_c = nc.inline_tensor(np.ones((8, 1), np.float32), name="o8C").ap()
    ones18f_c = nc.inline_tensor(np.ones((1, 8), np.float32), name="o18C").ap()
    onesbk_c = nc.inline_tensor(np.ones((1, 128), np.float32),
                                name="obkC").ap()
    mglob_c = nc.inline_tensor(mglob_np, name="mglobC").ap()
    bd64_c = nc.inline_tensor(bd64_np, name="bd64C").ap()
    iotaF_c = nc.inline_tensor(iotaF_np, name="iotaFC").ap()
    iotaC_c = nc.inline_tensor(iotaC_np, name="iotaCC").ap()
    cpb_c = nc.inline_tensor(cpb_np, name="cpbC").ap()
    o128_c = nc.inline_tensor(np.ones((128, 1), np.float32), name="o128C").ap()

    F = FusedCtx()
    F.SH_FLf = SH_FL
    F.SH_PR = SH_PR.rearrange("a k j -> (a k) j")
    F.SH_PRw = SH_PR.rearrange("a k j -> a (k j)")
    F.SH_XN = SH_XN
    F.SH_Y = SH_Y
    F.SH_SL = SH_SL
    F.SH_PM = SH_PM

    with ExitStack() as es:
        C = Ctx(nc, es)
        F.metaS = C.sb("metaS", [1, 96], I32)
        F.flagr = C.sb("flagr", [1, 2], I32)
        F.metaB = C.sb("metaB", [128, 96], F32)
        F.iotaC = C.sb("iotaCS", [128, 1], F32)
        F.ixn = C.sb("ixnS", [128, 16], I32)
        F.iss = C.sb("issS", [128, 2], I32)
        F.ipr = C.sb("iprS", [128, 2], I32)
        F.ifl = C.sb("iflS", [1, 8], I32)
        F.ipb = C.sb("ipbS", [128, 2], I32)
        F.islw = C.sb("islwS", [128, 4], I32)
        F.ifin = C.sb("ifinS", [128, 16], I32)
        F.iytf = C.sb("iytS", [128, 4 * NST], F32)
        idxf = C.sb("idxfS", [128, 4 * NST], F32)
        metaf = C.sb("metafS", [1, 96], F32)
        onesbk = C.sb("obkS", [1, 128], F32)
        cpbS = C.sb("cpbS", [128, 2], F32)

        s_mt = C.sem("mtF")
        s_mb = C.sem("mbF")
        s_ic = C.sem("icF")
        chv0 = Chain(C, "chvP0")

        with ExitStack() as psc:
            P = Ctx(nc, psc)
            ps_mb = P.ps("psmbF", [128, 96])
            with nc.Block() as blk:
                @blk.sync
                def _(sy):
                    sy.dma_start(out=F.metaS, in_=meta_d).then_inc(s_mt.h, 16)
                    sy.dma_start(out=onesbk, in_=onesbk_c).then_inc(s_mt.h, 16)
                    sy.dma_start(out=F.iotaC, in_=iotaC_c).then_inc(s_mt.h, 16)
                    sy.dma_start(out=cpbS, in_=cpb_c).then_inc(s_mt.h, 16)

                @blk.vector
                def _(v):
                    v.wait_ge(s_mt.h, 64)
                    ins = v.tensor_copy(out=metaf, in_=F.metaS)
                    s_mb.inc(ins)
                    v.wait_ge(s_mb.h, 2)
                    chv0.bar(v, v.tensor_copy(out=F.metaB, in_=ps_mb))
                    for i in range(16):
                        v.tensor_tensor(idxf[:, i:i + 1],
                                        F.metaB[:, 8 + i:9 + i], F.iotaC,
                                        ALU.add)
                    chv0.bar(v, v.engine_nop())
                    chv0.bar(v, v.tensor_copy(out=F.ixn, in_=idxf[:, 0:16]))
                    for sl in range(2):
                        v.scalar_tensor_tensor(idxf[:, sl:sl + 1],
                                               F.metaB[:, sl:sl + 1], 128.0,
                                               F.iotaC, ALU.mult, ALU.add)
                    chv0.bar(v, v.engine_nop())
                    chv0.bar(v, v.tensor_copy(out=F.iss, in_=idxf[:, 0:2]))
                    for sl in range(2):
                        v.tensor_tensor(idxf[:, sl:sl + 1],
                                        F.metaB[:, 6 + sl:7 + sl], F.iotaC,
                                        ALU.add)
                    chv0.bar(v, v.engine_nop())
                    chv0.bar(v, v.tensor_copy(out=F.ipr, in_=idxf[:, 0:2]))
                    for bar in range(4):
                        v.tensor_scalar(idxf[0:1, bar:bar + 1],
                                        F.metaB[0:1, 2:3], 16.0, float(bar),
                                        ALU.mult, ALU.add)
                        v.tensor_scalar(idxf[0:1, 4 + bar:5 + bar],
                                        F.metaB[0:1, 3:4], 16.0, float(bar),
                                        ALU.mult, ALU.add)
                    chv0.bar(v, v.engine_nop())
                    chv0.bar(v, v.tensor_copy(out=F.ifl, in_=idxf[0:1, 0:8]))
                    for b in range(2):
                        v.scalar_tensor_tensor(idxf[:, b:b + 1],
                                               F.metaB[:, 2:3], 64.0,
                                               cpbS[:, b:b + 1],
                                               ALU.mult, ALU.add)
                    chv0.bar(v, v.engine_nop())
                    chv0.bar(v, v.tensor_copy(out=F.ipb, in_=idxf[:, 0:2]))
                    for j in range(4):
                        chv0.bar(v, v.scalar_tensor_tensor(
                            idxf[:, j:j + 1], F.metaB[:, 48 + j:49 + j], 64.0,
                            F.iotaC, ALU.mult, ALU.add))
                        if j % 2 == 1:
                            chv0.bar(v, v.tensor_scalar(
                                idxf[:, j:j + 1], idxf[:, j:j + 1], -64.0,
                                None, ALU.add))
                    chv0.bar(v, v.tensor_copy(out=F.islw, in_=idxf[:, 0:4]))
                    for i in range(16):
                        v.scalar_tensor_tensor(idxf[:, i:i + 1],
                                               F.metaB[:, 56 + i:57 + i],
                                               128.0, F.iotaC,
                                               ALU.mult, ALU.add)
                    chv0.bar(v, v.engine_nop())
                    chv0.bar(v, v.tensor_copy(out=F.ifin, in_=idxf[:, 0:16]))
                    for j in range(4):
                        for t in range(NST):
                            chv0.bar(v, v.scalar_tensor_tensor(
                                idxf[:, NST * j + t:NST * j + t + 1],
                                F.metaB[:, 48 + j:49 + j], float(ECAP),
                                F.iotaC, ALU.mult, ALU.add))
                            chv0.bar(v, v.tensor_scalar(
                                idxf[:, NST * j + t:NST * j + t + 1],
                                idxf[:, NST * j + t:NST * j + t + 1],
                                float(128 * t), None, ALU.add))
                    chv0.bar(v, v.tensor_copy(out=F.iytf, in_=idxf))
                    s_ic.inc(v.engine_nop())

                @blk.tensor
                def _(t):
                    t.wait_ge(s_mb.h, 1)
                    s_mb.inc(t.matmul(ps_mb, onesbk, metaf,
                                      start=True, stop=True,
                                      skip_group_check=True))

                @blk.gpsimd
                def _(g):
                    g.wait_ge(s_ic.h, 1)
        nc.all_engine_barrier()
        if stop == "P0":
            return nc

        for sl in range(2):
            emit_L1(nc, xq2[sl], wq_d, wk_d, wv_d, qzP[sl], kfP[sl], ktP[sl],
                    vtP[sl], ssP[sl], ident_c, bp_c, E_c, pref=f"a{sl}_")
            clear_free_sems(nc)
        if stop == "A":
            return nc

        with ExitStack() as psc:
            P = Ctx(nc, psc)
            ssX = P.sb("ssX", [128, 1024], F32)
            s_xl = P.sem("xlS")
            s_ex = P.sem("exS")
            with nc.Block() as blk:
                @blk.sync
                def _(sy):
                    for sl in range(2):
                        sy.dma_start(out=ssX[:, 512 * sl:512 * (sl + 1)],
                                     in_=ssP[sl]).then_inc(s_xl.h, 16)

                @blk.gpsimd
                def _(g):
                    g.wait_ge(s_xl.h, 32)
                    for sl in range(2):
                        g.indirect_dma_start(
                            SH_S.rearrange("q p s -> (q p) s"),
                            IndirectOffsetOnAxis(F.iss[:, sl:sl + 1], 0),
                            ssX[:, 512 * sl:512 * (sl + 1)], None,
                            oob_is_err=False).then_inc(s_ex.h, 16)
                        s_ex.v += 16
                    g.wait_ge(s_ex.h, s_ex.v)
        clear_free_sems(nc)
        emit_pair_barrier(nc, C, F, 0)
        if stop == "X":
            return nc

        def mk_xn2_wr():
            def cb(g, i, src):
                return g.indirect_dma_start(
                    SH_XN, IndirectOffsetOnAxis(F.ixn[:, i:i + 1], 0),
                    src, None, oob_is_err=False)
            return cb

        def mk_probs_wr(sl):
            def cb(g, src):
                return g.indirect_dma_start(
                    F.SH_PRw, IndirectOffsetOnAxis(F.ipr[0:NE, sl:sl + 1], 0),
                    src, None, oob_is_err=False)
            return cb

        for sl in range(2):
            emit_L2(nc, xq2[sl], qzP[sl], kfP[sl], ktP[sl], vtP[sl], SH_S,
                    msks_d[sl], wo_d, bo_d, gw_d, r1P, None, None,
                    ident_c, maskT_c, ones_c, ones8f_c, ones18f_c,
                    NSRC=4, pref=f"b{sl}_",
                    xn2_wr=mk_xn2_wr(), probs_wr=mk_probs_wr(sl),
                    write_r1=(sl == 0), write_xn2=(sl == 0))
            clear_free_sems(nc)
        emit_pair_barrier(nc, C, F, 1)
        if stop == "B":
            return nc

        idxiA = C.sb("idxiA", [128, 4 * NST], I32)
        with ExitStack() as psc:
            P = Ctx(nc, psc)
            emit_phaseC(nc, P, F, stP, selh_d, myh_d, hbase_d, mglob_c,
                        bd64_c, iotaF_c, iotaC_c, o128_c, idxiA,
                        nphC=FUSED_NPHC)
        clear_free_sems(nc)
        if stop == "C":
            return nc

        for j in range(4):
            with ExitStack() as psc:
                P = Ctx(nc, psc)
                emit_ffn_expert(nc, P, F, j, idxiA, w1e_d, w2e_d, ident_c)
            clear_free_sems(nc)
        emit_pair_barrier(nc, C, F, 2)
        if stop == "F":
            return nc

        with ExitStack() as psc:
            P = Ctx(nc, psc)
            emit_final(nc, P, F, r1P, out_o)
    return nc


def emit_phaseC(nc, C, F, stP, selh_d, myh_d, hbase_d, mglob_c, bd64_c,
                iotaF_c, iotaC_c, o128_c, idxiA, nphC=3):
    SH_PR, SH_SL, SH_PM = F.SH_PR, F.SH_SL, F.SH_PM
    prB = [C.sb(f"prB{b}", [128, 128], F32) for b in range(2)]
    mS = [C.sb(f"mSc{b}", [128, 128], F32) for b in range(2)]
    r0 = [C.sb(f"r0c{b}", [128, 128], F32) for b in range(2)]
    rsl = [C.sb(f"rslc{b}", [128, 128], F32) for b in range(2)]
    pm = [C.sb(f"pmc{b}", [128, 128], F32) for b in range(2)]
    sof = [C.sb(f"sofc{b}", [128, 128], F32) for b in range(2)]
    tmp = C.sb("tmpc", [128, 128], F32)
    zsc = C.sb("zscc", [128, 128], F32)
    th2 = C.sb("th2c", [128, 2], F32)
    thm = C.sb("thmc", [128, 2], F32)
    c2 = C.sb("c2c", [128, 2], F32)
    cnt2 = C.sb("cnt2c", [128, 2], F32)
    bgt = C.sb("bgtc", [128, 2], F32)
    off2 = C.sb("off2c", [128, 4], F32)
    mgl = C.sb("mglS", [128, 128], F32)
    bd64 = C.sb("bd64S", [128, 128], F32)
    selh = C.sb("selhS", [128, 128], F32)
    iotaF = C.sb("iotaFS", [128, 128], F32)
    o128 = C.sb("o128S", [128, 1], F32)
    myh = C.sb("myhS", [128, 1], F32)
    hbase = C.sb("hbaseS", [128, 1], F32)
    sekA = C.sb("sekAc", [128, NST], F32)
    indA = [C.sb(f"indA{b}", [128, NST * 128], F32) for b in range(2)]
    nbf = C.sb("nbfc", [128, 4 * NST], F32)
    nrow = C.sb("nrowc", [128, 4 * NST], F32)
    nri = C.sb("nric", [128, 4 * NST], I32)
    scolA = C.sb("scolA", [128, NST], F32)
    srow = C.sb("srowc", [128, 4 * 128], F32)
    wcmp = C.sb("wcmpc", [128, 128], F32)
    wcol = C.sb("wcolc", [128, 4 * NST], F32)
    idxf = C.sb("idxfc", [128, 4 * NST], F32)

    s_ld = C.sem("ldC")
    s_pg = C.sem("pgC")
    s_cd = C.sem("cdC")
    s_bi = C.sem("biC")
    s_se = C.sem("seC")
    s_pe2 = C.sem("pe2C")
    s_sl2 = C.sem("sl2C")
    s_stw = C.sem("stwC")
    s_ind = C.sem("indC")
    s_nb = C.sem("nbC")
    s_nr = C.sem("nrC")
    s_sg = [C.sem(f"sgC{n}") for n in range(4 * NST)]
    s_wc = C.sem("wcC")
    s_shw = C.sem("shwC")
    s_z = C.sem("zC")
    chv = Chain(C, "chvPC")

    ps_cnt = C.ps("pcntC", [128, 2])
    ps_o1 = C.ps("po1C", [128, 2])
    ps_o2 = C.ps("po2C", [128, 2])
    ps_nb = [C.ps(f"pnbC{j}", [128, 512]) for j in range(4)]

    with nc.Block() as blk:
        @blk.sync
        def _(sy):
            sy.dma_start(out=mgl, in_=mglob_c).then_inc(s_ld.h, 16)
            sy.dma_start(out=bd64, in_=bd64_c).then_inc(s_ld.h, 16)
            sy.dma_start(out=selh, in_=selh_d).then_inc(s_ld.h, 16)
            sy.dma_start(out=iotaF, in_=iotaF_c).then_inc(s_ld.h, 16)
            sy.dma_start(out=o128, in_=o128_c).then_inc(s_ld.h, 16)
            sy.dma_start(out=myh, in_=myh_d).then_inc(s_ld.h, 16)
            sy.dma_start(out=hbase, in_=hbase_d).then_inc(s_ld.h, 16)
            s_ld.v += 7 * 16

        @blk.gpsimd
        def _(g):
            g.memset(zsc, 0.0)
            g.memset(th2, 0.5).then_inc(s_z.h, 1)
            for b in range(2):
                g.indirect_dma_start(
                    prB[b], None, SH_PR,
                    IndirectOffsetOnAxis(F.ipb[:, b:b + 1], 0),
                    oob_is_err=False).then_inc(s_pg.h, 16)
                s_pg.v += 16
            if nphC < 2:
                return
            g.wait_ge(s_sl2.h, 3)
            for j in range(4):
                g.dma_start(out=stP[64 * j:64 * (j + 1), :],
                            in_=rsl[j // 2][64 * (j % 2):64 * (j % 2) + 64, :]
                            ).then_inc(s_stw.h, 16)
                s_stw.v += 16
            g.wait_ge(s_stw.h, s_stw.v)
            if nphC >= 3:
                g.wait_ge(s_nr.h, 1)
            for n in range(4 * NST if nphC >= 3 else 0):
                if n >= 4:
                    g.wait_ge(s_wc.h, n - 3)
                g.indirect_dma_start(
                    srow[:, 128 * (n % 4):128 * (n % 4) + 128], None,
                    stP, IndirectOffsetOnAxis(nri[:, n:n + 1], 0),
                    oob_is_err=False).then_inc(s_sg[n].h, 16)
                s_sg[n].v += 16
            g.wait_ge(s_sl2.h, 4)
            for j in range(4 if FUSED_SLPM else 0):
                b, jj = j // 2, j % 2
                re64 = g.value_load(F.metaS[0:1, 72 + j:73 + j])
                g.dma_start(out=SH_SL[bass.ds(re64, 64), :],
                            in_=sof[b][64 * jj:64 * jj + 64, :]
                            ).then_inc(s_shw.h, 16)
                s_shw.v += 16
                g.dma_start(out=SH_PM[bass.ds(re64, 64), :],
                            in_=pm[b][64 * jj:64 * jj + 64, :]
                            ).then_inc(s_shw.h, 16)
                s_shw.v += 16
            g.wait_ge(s_shw.h, s_shw.v)

        @blk.tensor
        def _(t):
            for it in range(NITER_BIS):
                t.wait_ge(s_cd.h, it + 1)
                s_bi.inc(t.matmul(ps_cnt, bd64, c2, start=True, stop=True,
                                  skip_group_check=True))
            if nphC < 2:
                return
            t.wait_ge(s_se.h, 1)
            s_pe2.inc(t.matmul(ps_o1, selh, c2, start=True, stop=True,
                               skip_group_check=True))
            s_pe2.inc(t.matmul(ps_o2, mgl, c2, start=True, stop=True,
                               skip_group_check=True))
            if nphC < 3:
                return
            t.wait_ge(s_ind.h, 1)
            for k in range(NST):
                for j in range(4):
                    b, jj = j // 2, j % 2
                    s_nb.inc(t.matmul(
                        ps_nb[j][:, k:k + 1],
                        indA[b][64 * jj:64 * jj + 64, 128 * k:128 * (k + 1)],
                        o128[64 * jj:64 * jj + 64, :],
                        start=True, stop=True, skip_group_check=True))

        @blk.vector
        def _(v):
            v.wait_ge(s_ld.h, s_ld.v)
            v.wait_ge(s_pg.h, s_pg.v)
            v.wait_ge(s_z.h, 1)
            for it in range(NITER_BIS):
                step = 0.25 * (0.5 ** it)
                ins = None
                for b in range(2):
                    chv.bar(v, v.tensor_scalar(mS[b], prB[b], th2[:, b:b + 1],
                                               None, ALU.is_gt))
                    ins = v.reduce_sum(c2[:, b:b + 1], mS[b], axis=AXX)
                    if b == 0:
                        chv.bar(v, ins)
                s_cd.inc(ins)
                v.wait_ge(s_bi.h, it + 1)
                chv.bar(v, v.tensor_copy(out=cnt2, in_=ps_cnt))
                chv.bar(v, v.tensor_scalar(bgt, cnt2, float(CAP), None,
                                           ALU.is_gt))
                chv.bar(v, v.tensor_scalar(thm, th2, step, None,
                                           ALU.subtract))
                chv.bar(v, v.scalar_tensor_tensor(th2, bgt, 2.0 * step, thm,
                                                  ALU.mult, ALU.add))
            if nphC < 2:
                return
            for b in range(2):
                chv.bar(v, v.tensor_scalar(mS[b], prB[b], th2[:, b:b + 1],
                                           None, ALU.is_gt))
                ins = v.reduce_sum(c2[:, b:b + 1], mS[b], axis=AXX)
                chv.bar(v, ins)
            s_se.inc(v.engine_nop())
            for b in range(2):
                chv.bar(v, v.tensor_tensor_scan(r0[b], mS[b], zsc, 0.0,
                                                ALU.add, ALU.add))
            v.wait_ge(s_pe2.h, 2)
            chv.bar(v, v.tensor_copy(out=off2[:, 0:2], in_=ps_o1))
            chv.bar(v, v.tensor_copy(out=off2[:, 2:4], in_=ps_o2))
            for b in range(2):
                chv.bar(v, v.tensor_scalar(rsl[b], r0[b], off2[:, b:b + 1],
                                           None, ALU.add))
                chv.bar(v, v.tensor_scalar(tmp, r0[b], off2[:, 2 + b:3 + b],
                                           None, ALU.add))
                chv.bar(v, v.tensor_scalar(tmp, tmp, float(CAP), None,
                                           ALU.is_le))
                chv.bar(v, v.tensor_mul(pm[b], prB[b], mS[b]))
                chv.bar(v, v.tensor_mul(pm[b], pm[b], tmp))
                chv.bar(v, v.tensor_scalar(tmp, rsl[b], float(ECAP), None,
                                           ALU.is_le))
                ins = v.tensor_mul(pm[b], pm[b], tmp)
                s_sl2.inc(ins)
                chv.bar(v, v.tensor_scalar(sof[b], rsl[b], 1.0 + BIGIDX,
                                           None, ALU.subtract))
                chv.bar(v, v.tensor_mul(sof[b], sof[b], mS[b]))
                ins = v.tensor_scalar(sof[b], sof[b], BIGIDX, None, ALU.add)
                s_sl2.inc(ins)
            if nphC < 3:
                return
            for b in range(2):
                for k in range(NST):
                    chv.bar(v, v.tensor_scalar(sekA[:, k:k + 1],
                                               rsl[b][:, 127:128],
                                               float(128 * k), None,
                                               ALU.subtract))
                    chv.bar(v, v.tensor_scalar(
                        indA[b][:, 128 * k:128 * (k + 1)], iotaF,
                        sekA[:, k:k + 1], None, ALU.is_ge))
                    chv.bar(v, v.tensor_scalar(
                        indA[b][:, 128 * k:128 * (k + 1)],
                        indA[b][:, 128 * k:128 * (k + 1)],
                        myh[:, 0:1], None, ALU.mult))
            s_ind.inc(v.engine_nop())
            v.wait_ge(s_nb.h, 4 * NST)
            for j in range(4):
                chv.bar(v, v.tensor_copy(out=nbf[:, NST * j:NST * (j + 1)],
                                         in_=ps_nb[j][:, 0:NST]))
            for j in range(4):
                chv.bar(v, v.tensor_scalar(nrow[:, NST * j:NST * (j + 1)],
                                           nbf[:, NST * j:NST * (j + 1)],
                                           hbase[:, 0:1], float(64 * j),
                                           ALU.add, ALU.add))
            chv.bar(v, v.tensor_scalar_min(nrow, nrow, 255.0))
            chv.bar(v, v.tensor_copy(out=nri, in_=nrow))
            s_nr.inc(v.engine_nop())
            for k in range(NST):
                chv.bar(v, v.tensor_scalar(scolA[:, k:k + 1], F.iotaC,
                                           float(128 * k), None, ALU.add))
            for n in range(4 * NST):
                k = n % NST
                v.wait_ge(s_sg[n].h, 16)
                chv.bar(v, v.tensor_scalar(
                    wcmp, srow[:, 128 * (n % 4):128 * (n % 4) + 128],
                    scolA[:, k:k + 1], None, ALU.is_le))
                chv.bar(v, v.reduce_sum(wcol[:, n:n + 1], wcmp, axis=AXX))
                chv.bar(v, v.scalar_tensor_tensor(idxf[:, n:n + 1],
                                                  nbf[:, n:n + 1], 128.0,
                                                  wcol[:, n:n + 1],
                                                  ALU.mult, ALU.add))
                ins = v.tensor_scalar_min(idxf[:, n:n + 1], idxf[:, n:n + 1],
                                          float(2 * SL - 1))
                s_wc.inc(ins)
            v.wait_ge(s_wc.h, 4 * NST)
            chv.bar(v, v.tensor_scalar_max(idxf, idxf, 0.0))
            chv.bar(v, v.tensor_copy(out=idxiA, in_=idxf))


def emit_ffn_expert(nc, C, F, j, idxiA, w1e_d, w2e_d, ident_c):
    SH_XN = F.SH_XN
    NTG = 4
    xinG = [C.sb(f"xg{j}_{i}", [128, D], BF16) for i in range(2)]
    xinT = [C.sb(f"xt{j}_{k}", [128, 512], BF16) for k in range(NF)]
    hfm = [C.sb(f"hf{j}_{m}", [128, 512], BF16) for m in range(32)]
    w1b = C.sb(f"w1b{j}", [128, NF * HID], BF16)
    w2b = C.sb(f"w2b{j}", [128, 32 * 1024], BF16)
    ybuf = [C.sb(f"yb{j}_{i}", [128, 512], F32) for i in range(2)]
    ident = C.sb(f"idF{j}", [128, 128], BF16)
    iy2f = C.sb(f"iy2f{j}", [128, 2 * NST], F32)
    iy2 = C.sb(f"iy2{j}", [128, 2 * NST], I32)

    s_c = C.sem(f"cF{j}")
    s_xg = [C.sem(f"xgF{j}_{k}") for k in range(4 * NTG)]
    s_tp = C.sem(f"tpF{j}")
    s_te = C.sem(f"teF{j}")
    s_w1 = [C.sem(f"w1F{j}_{cc}") for cc in range(8)]
    s_w2 = [C.sem(f"w2F{j}_{cc}") for cc in range(8)]
    s_h = C.sem(f"hF{j}")
    s_he = C.sem(f"heF{j}")
    s_y = C.sem(f"yF{j}")
    s_ye = C.sem(f"yeF{j}")
    s_yo = [C.sem(f"yoF{j}_{i}") for i in range(2)]
    s_iy = C.sem(f"iyF{j}")
    chf = Chain(C, f"chF{j}")
    SH_Y2 = F.SH_Y.rearrange("r (d c) -> (r d) c", c=512)

    ps_tp = [C.ps(f"ptpF{j}_{i}", [128, 1024], BF16) for i in range(2)]
    ps_h = [C.ps(f"phF{j}_{i}", [128, 512]) for i in range(2)]
    ps_y = [C.ps(f"pyF{j}_{i}", [128, 512]) for i in range(2)]

    with nc.Block() as blk:
        @blk.gpsimd
        def _(g):
            def gather(t):
                if t >= 2:
                    g.wait_ge(s_tp.h, t - 1)
                g.indirect_dma_start(
                    xinG[t % 2], None, SH_XN,
                    IndirectOffsetOnAxis(idxiA[:, NST * j + t:
                                               NST * j + t + 1], 0),
                    oob_is_err=False).then_inc(s_xg[t].h, 16)
                s_xg[t].v += 16

            def scatters(tg):
                for tt in range(4):
                    for dh in range(2):
                        n = 8 * tg + 2 * tt + dh
                        g.wait_ge(s_ye.h, n + 1)
                        g.indirect_dma_start(
                            SH_Y2,
                            IndirectOffsetOnAxis(
                                iy2[:, 2 * (4 * tg + tt) + dh:
                                    2 * (4 * tg + tt) + dh + 1], 0),
                            ybuf[n % 2], None,
                            oob_is_err=False).then_inc(s_yo[n % 2].h, 16)
                        s_yo[n % 2].v += 16

            g.wait_ge(s_iy.h, 1)
            for t in range(4):
                gather(t)
            for cc in range(8):
                g.dma_start(
                    out=w1b.rearrange("p (k m) -> p k m", m=HID)
                    [:, :, 512 * cc:512 * (cc + 1)],
                    in_=w1e_d[j].rearrange("(k p) m -> p k m", p=128)
                    [:, :, 512 * cc:512 * (cc + 1)]).then_inc(s_w1[cc].h, 16)
                s_w1[cc].v += 16
            for cc in range(8):
                g.dma_start(
                    out=w2b.rearrange("p (m c) -> p m c", c=1024)
                    [:, 4 * cc:4 * (cc + 1), :],
                    in_=w2e_d[j].rearrange("(m p) c -> p m c", p=128)
                    [:, 4 * cc:4 * (cc + 1), :]).then_inc(s_w2[cc].h, 16)
                s_w2[cc].v += 16
            for tg in range(NTG):
                for t in range(4 * (tg + 1), min(4 * (tg + 2), 16)):
                    gather(t)
                scatters(tg)
            g.wait_ge(s_yo[0].h, s_yo[0].v)
            g.wait_ge(s_yo[1].h, s_yo[1].v)

        @blk.sync
        def _(sy):
            sy.dma_start(out=ident, in_=ident_c).then_inc(s_c.h, 16)

        @blk.vector
        def _(v):
            for t in range(NST):
                for dh in range(2):
                    chf.bar(v, v.tensor_scalar(
                        iy2f[:, 2 * t + dh:2 * t + dh + 1],
                        F.iytf[:, NST * j + t:NST * j + t + 1],
                        2.0, float(dh), ALU.mult, ALU.add))
            chf.bar(v, v.tensor_copy(out=iy2, in_=iy2f))
            v.engine_nop().then_inc(s_iy.h, 1)

        @blk.tensor
        def _(t):
            t.wait_ge(s_c.h, 16)

            def transposes(tg):
                for tt in range(4):
                    ti = 4 * tg + tt
                    t.wait_ge(s_xg[ti].h, 16)
                    if ti >= 2:
                        t.wait_ge(s_te.h, ti - 1)
                    for kk in range(NF):
                        ins = t.transpose(
                            ps_tp[ti % 2][:, 128 * kk:128 * (kk + 1)],
                            xinG[ti % 2][:, 128 * kk:128 * (kk + 1)], ident)
                    s_tp.inc(ins)

            transposes(0)
            hg = 0
            yg = 0
            for tg in range(NTG):
                t.wait_ge(s_te.h, 4 * (tg + 1))
                for m in range(32):
                    if m % 4 == 0 and tg == 0:
                        t.wait_ge(s_w1[m // 4].h, 16)
                    if hg >= 2:
                        t.wait_ge(s_he.h, hg - 1)
                    for k in range(NF):
                        ins = t.matmul(
                            ps_h[hg % 2],
                            w1b[:, HID * k + 128 * m:HID * k + 128 * (m + 1)],
                            xinT[k], start=(k == 0), stop=(k == NF - 1))
                    s_h.inc(ins)
                    hg += 1
                if tg + 1 < NTG:
                    transposes(tg + 1)
                for tt in range(4):
                    for dh in range(2):
                        if yg >= 2:
                            t.wait_ge(s_ye.h, yg - 1)
                        for m in range(32):
                            if m % 4 == 0 and tg == 0 and tt == 0 and dh == 0:
                                t.wait_ge(s_w2[m // 4].h, 16)
                            ins = t.matmul(
                                ps_y[yg % 2],
                                hfm[m][:, 128 * tt:128 * (tt + 1)],
                                w2b[:, 1024 * m + 512 * dh:
                                    1024 * m + 512 * (dh + 1)],
                                start=(m == 0), stop=(m == 31))
                        s_y.inc(ins)
                        yg += 1

        @blk.scalar
        def _(s):
            def trevacs(tg):
                for tt in range(4):
                    ti = 4 * tg + tt
                    s.wait_ge(s_tp.h, ti + 1)
                    for kk in range(NF):
                        ins = s.activation(
                            out=xinT[kk][:, 128 * tt:128 * (tt + 1)],
                            in_=ps_tp[ti % 2][:, 128 * kk:128 * (kk + 1)],
                            func=AF.Copy)
                    s_te.inc(ins)

            trevacs(0)
            hg = 0
            yg = 0
            for tg in range(NTG):
                for m in range(32):
                    s.wait_ge(s_h.h, hg + 1)
                    s_he.inc(s.activation(out=hfm[m], in_=ps_h[hg % 2],
                                          func=AF.Gelu))
                    hg += 1
                if tg + 1 < NTG:
                    trevacs(tg + 1)
                for tt in range(4):
                    for dh in range(2):
                        s.wait_ge(s_y.h, yg + 1)
                        if yg >= 2:
                            s.wait_ge(s_yo[yg % 2].h, 16 * (yg // 2))
                        s_ye.inc(s.activation(out=ybuf[yg % 2],
                                              in_=ps_y[yg % 2],
                                              func=AF.Copy))
                        yg += 1

    return


def emit_final(nc, C, F, r1P, out_o):
    SH_SLf = F.SH_SL.rearrange("r c -> (r c)").rearrange("(r o) -> r o", o=1)
    SH_PMf = F.SH_PM.rearrange("r c -> (r c)").rearrange("(r o) -> r o", o=1)
    acc = [C.sb(f"acL{i}", [128, D], F32) for i in range(NT)]
    scr = [C.sb(f"scL{i}", [128, D], F32) for i in range(2)]
    slc = [C.sb(f"slcL{i}", [128, NE], F32) for i in range(2)]
    pmc = [C.sb(f"pmcL{i}", [128, NE], F32) for i in range(2)]
    sly = [C.sb(f"slyL{i}", [128, NE], F32) for i in range(2)]
    sli = [C.sb(f"sliL{i}", [128, NE], I32) for i in range(2)]
    ifine = [C.sb(f"ifineL{i}", [128, NE], I32) for i in range(2)]
    ifinf = C.sb("ifinfL", [128, NE], F32)

    s_al = [C.sem(f"alL{i}") for i in range(NT)]
    s_ms = C.sem("msL")
    s_ix = C.sem("ixL")
    s_sp = C.sem("spL")
    s_cv = C.sem("cvL")
    s_g = [C.sem(f"gL{e}") for e in range(NE)]
    s_ac = C.sem("aqL")
    s_oo = [C.sem(f"ooL{i}") for i in range(2)]
    chi = Chain(C, "chiL")

    with nc.Block() as blk:
        @blk.sync
        def _(sy):
            for i in range(NT):
                sy.dma_start(out=acc[i], in_=r1P[128 * i:128 * (i + 1), :]
                             ).then_inc(s_al[i].h, 16)
                s_al[i].v += 16

        @blk.vector
        def _(v):
            for i in range(NT):
                r = i % 2
                if i >= 2:
                    v.wait_ge(s_sp.h, 16 * 16 * (i - 1))
                for e in range(NE):
                    chi.bar(v, v.tensor_scalar(ifinf[:, e:e + 1],
                                               F.ifin[:, i:i + 1],
                                               float(8192 * e), None,
                                               ALU.add))
                chi.bar(v, v.tensor_copy(out=ifine[r], in_=ifinf))
                v.engine_nop().then_inc(s_ix.h, 1)
                v.wait_ge(s_sp.h, 16 * 16 * (i + 1))
                chi.bar(v, v.tensor_scalar_min(slc[r], slc[r],
                                               float(ECAP - 1)))
                for e in range(NE):
                    chi.bar(v, v.tensor_scalar(sly[r][:, e:e + 1],
                                               slc[r][:, e:e + 1],
                                               float(ECAP * e), None,
                                               ALU.add))
                chi.bar(v, v.tensor_copy(out=sli[r], in_=sly[r]))
                v.engine_nop().then_inc(s_cv.h, 1)
                for e in range(NE):
                    n = NE * i + e
                    v.wait_ge(s_g[e].h, 16 * (i + 1))
                    if e == 0:
                        v.wait_ge(s_al[i].h, 16)
                    ins = v.scalar_tensor_tensor(
                        acc[i], scr[n % 2], pmc[r][:, e:e + 1],
                        acc[i], ALU.mult, ALU.add)
                    s_ac.inc(ins)
                    v.wait_ge(s_ac.h, s_ac.v)

        @blk.gpsimd
        def _(g):
            g.memset(scr[0], 0.0)
            g.memset(scr[1], 0.0).then_inc(s_ms.h, 1)
            g.wait_ge(s_ms.h, 1)
            for i in range(NT):
                r = i % 2
                g.wait_ge(s_ix.h, i + 1)
                if i >= 2:
                    g.wait_ge(s_ac.h, NE * (i - 1))
                for e in range(NE):
                    g.indirect_dma_start(
                        slc[r][:, e:e + 1], None, SH_SLf,
                        IndirectOffsetOnAxis(ifine[r][:, e:e + 1], 0),
                        oob_is_err=False).then_inc(s_sp.h, 16)
                    g.indirect_dma_start(
                        pmc[r][:, e:e + 1], None, SH_PMf,
                        IndirectOffsetOnAxis(ifine[r][:, e:e + 1], 0),
                        oob_is_err=False).then_inc(s_sp.h, 16)
                g.wait_ge(s_cv.h, i + 1)
                for e in range(NE):
                    n = NE * i + e
                    if n >= 2:
                        g.wait_ge(s_ac.h, n - 1)
                    g.indirect_dma_start(
                        scr[n % 2], None, F.SH_Y,
                        IndirectOffsetOnAxis(sli[r][:, e:e + 1], 0),
                        oob_is_err=False).then_inc(s_g[e].h, 16)
                    s_g[e].v += 16

        @blk.scalar
        def _(s):
            for i in range(NT):
                s.wait_ge(s_ac.h, NE * (i + 1))
                s.dma_start(out=out_o[128 * i:128 * (i + 1), :],
                            in_=acc[i]).then_inc(s_oo[i % 2].h, 16)
                s_oo[i % 2].v += 16
            s.wait_ge(s_oo[0].h, s_oo[0].v)
            s.wait_ge(s_oo[1].h, s_oo[1].v)


def _stack(arrs):
    return np.ascontiguousarray(np.stack(arrs))


def kernel_multi(L0_attn_x, L0_attn_Wq, L0_attn_Wk, L0_attn_Wv, L0_attn_Wo,
                 L0_attn_bo, L0_ffn_gate_w, L0_ffn_w1, L0_ffn_w2):
    x = np.ascontiguousarray(np.asarray(L0_attn_x))
    Wq = np.ascontiguousarray(np.asarray(L0_attn_Wq))
    Wk = np.ascontiguousarray(np.asarray(L0_attn_Wk))
    Wv = np.ascontiguousarray(np.asarray(L0_attn_Wv))
    Wo = np.ascontiguousarray(np.asarray(L0_attn_Wo))
    bo = np.ascontiguousarray(np.asarray(L0_attn_bo)).reshape(1, D)
    gw = np.ascontiguousarray(np.asarray(L0_ffn_gate_w))
    w1 = np.ascontiguousarray(np.asarray(L0_ffn_w1))
    w2 = np.ascontiguousarray(np.asarray(L0_ffn_w2))
    cores = list(range(NCORE))

    def xsl(c):
        return np.ascontiguousarray(x[c // 4, SL * (c % 4):SL * (c % 4 + 1)])

    nc1 = build_L1()
    im1 = [{"xs": xsl(c), "wq": Wq, "wk": Wk, "wv": Wv} for c in cores]
    r1 = run_bass_kernel_spmd(nc1, im1, core_ids=cores).results

    sall = _stack([r1[c]["ssum"] for c in cores])
    nc2 = build_L2()
    im2 = []
    for c in cores:
        msk = np.zeros((128, NCORE), np.float32)
        msk[:, 4 * (c // 4):c] = 1.0
        im2.append({"xs": xsl(c), "qz": r1[c]["qz"], "kf": r1[c]["kf"],
                    "ktm": r1[c]["ktm"], "vtm": r1[c]["vtm"], "sall": sall,
                    "msk": msk, "wo": Wo, "bo": bo, "gw": gw})
    r2 = run_bass_kernel_spmd(nc2, im2, core_ids=cores).results

    xn2full = [np.ascontiguousarray(
        np.concatenate([r2[4 * b + j]["xn2"] for j in range(4)], axis=0))
        for b in range(B)]
    probsfull = [np.ascontiguousarray(
        np.concatenate([r2[4 * b + j]["probs"] for j in range(4)], axis=1))
        for b in range(B)]
    nc3 = build_L3()
    r3 = []
    for b in range(B):
        im3 = [{"xn2f": xn2full[b],
                "prow": np.ascontiguousarray(probsfull[b][c:c + 1]),
                "w1": np.ascontiguousarray(w1[c]),
                "w2": np.ascontiguousarray(w2[c])} for c in cores]
        r3.append(run_bass_kernel_spmd(nc3, im3, core_ids=cores).results)

    nc4 = build_L4()
    acc = [np.ascontiguousarray(r2[c]["r1"]) for c in cores]
    for l in range(4):
        im4 = []
        for c in cores:
            b = c // 4
            t0 = SL * (c % 4)
            slotp = _stack([r3[b][2 * l + e]["slot"][0][t0:t0 + SL]
                            for e in range(2)])
            prb2 = np.ascontiguousarray(r2[c]["probs"][2 * l:2 * l + 2])
            im4.append({"acc": acc[c], "Ye0": r3[b][2 * l]["Y"],
                        "Ye1": r3[b][2 * l + 1]["Y"],
                        "slotp": slotp, "prb2": prb2})
        r4 = run_bass_kernel_spmd(nc4, im4, core_ids=cores).results
        acc = [r4[c]["accout"] for c in cores]

    out = np.empty((B, T, D), np.float32)
    for c in cores:
        out[c // 4, SL * (c % 4):SL * (c % 4 + 1)] = acc[c]
    return out


def fused_inputs(x, Wq, Wk, Wv, Wo, bo, gw, w1, w2, nonce=None):
    if nonce is None:
        nonce = int(np.random.default_rng().integers(1, 2 ** 30))
    ims = []
    for c in range(NCORE):
        L = host_core_layout(c, nonce)
        b = L["b"]
        xq2 = np.ascontiguousarray(np.stack(
            [x[b, SL * L["q_own"]:SL * (L["q_own"] + 1)],
             x[b, SL * L["q_other"]:SL * (L["q_other"] + 1)]]))
        es = L["r"] * 4
        ims.append({
            "xq2": xq2, "wq": Wq, "wk": Wk, "wv": Wv, "wo": Wo,
            "bo": bo, "gw": gw,
            "w1e": np.ascontiguousarray(w1[es:es + 4]),
            "w2e": np.ascontiguousarray(w2[es:es + 4]),
            "msks": L["msks"], "meta": L["meta"], "selh": L["selh"],
            "myh": L["myh"], "hbase": L["hbase"]})
    return ims


def kernel(L0_attn_x, L0_attn_Wq, L0_attn_Wk, L0_attn_Wv, L0_attn_Wo,
           L0_attn_bo, L0_ffn_gate_w, L0_ffn_w1, L0_ffn_w2):
    x = np.ascontiguousarray(np.asarray(L0_attn_x))
    Wq = np.ascontiguousarray(np.asarray(L0_attn_Wq))
    Wk = np.ascontiguousarray(np.asarray(L0_attn_Wk))
    Wv = np.ascontiguousarray(np.asarray(L0_attn_Wv))
    Wo = np.ascontiguousarray(np.asarray(L0_attn_Wo))
    bo = np.ascontiguousarray(np.asarray(L0_attn_bo)).reshape(1, D)
    gw = np.ascontiguousarray(np.asarray(L0_ffn_gate_w))
    w1 = np.ascontiguousarray(np.asarray(L0_ffn_w1))
    w2 = np.ascontiguousarray(np.asarray(L0_ffn_w2))
    cores = list(range(NCORE))
    nc = build_fused()
    ims = fused_inputs(x, Wq, Wk, Wv, Wo, bo, gw, w1, w2)
    res = run_bass_kernel_spmd(nc, ims, core_ids=cores).results
    out = np.empty((B, T, D), np.float32)
    for c in cores:
        L = host_core_layout(c, 0)
        out[L["b"], SL * L["q_own"]:SL * (L["q_own"] + 1)] = res[c]["out"]
    return out
